# version 44 (final)
# baseline (speedup 1.0000x reference)
import numpy as np
from contextlib import ExitStack

import concourse.bass as bass
import concourse.mybir as mybir
from concourse.bass_utils import run_bass_kernel_spmd

F32 = mybir.dt.float32
F32R = mybir.dt.float32r
AF = mybir.ActivationFunctionType
ALU = mybir.AluOpType

P = 128
DM = 1024
FC = 8
T = 2048
TT = 512
NT = 4
H = 16
D = 64
NC = 8
LN2 = 0.6931471805599453
INV_LN2 = 1.4426950408889634
C_EPS_LN, C_EPS_DEN, C_NLN2, C_PLN2, C_M7LN2 = 0, 1, 2, 3, 4
AOFF = [0, 512, 896, 1152]
AW = [512, 384, 256, 128]


def build_nc():
    nc = bass.Bass("TRN2", target_bir_lowering=False)

    xin = nc.dram_tensor("xin", [T, DM], F32, kind="ExternalInput").ap()
    wq = nc.dram_tensor("wq", [DM, DM], F32, kind="ExternalInput").ap()
    wk = nc.dram_tensor("wk", [DM, DM], F32, kind="ExternalInput").ap()
    wv = nc.dram_tensor("wv", [DM, DM], F32, kind="ExternalInput").ap()
    wo = nc.dram_tensor("wo", [DM, DM], F32, kind="ExternalInput").ap()
    bwT = nc.dram_tensor("bwT", [DM, DM], F32, kind="ExternalInput").ap()
    bb8 = nc.dram_tensor("bb8", [1, DM], F32, kind="ExternalInput").ap()
    ra = nc.dram_tensor("ra", [P, 64], F32, kind="ExternalInput").ap()
    rwp = nc.dram_tensor("rwp", [P, 64], F32, kind="ExternalInput").ap()
    rb = nc.dram_tensor("rb", [8, 1], F32, kind="ExternalInput").ap()
    lb = nc.dram_tensor("lb", [64, DM], F32, kind="ExternalInput").ap()
    prj = nc.dram_tensor("prj", [P, D], F32, kind="ExternalInput").ap()
    mskT = nc.dram_tensor("mskT", [P, P], F32, kind="ExternalInput").ap()
    idn = nc.dram_tensor("idn", [P, P], F32, kind="ExternalInput").ap()
    onesw = nc.dram_tensor("onesw", [P, TT], F32, kind="ExternalInput").ap()
    repg = nc.dram_tensor("repg", [40, 64], F32, kind="ExternalInput").ap()
    repm = nc.dram_tensor("repm", [40, 64], F32, kind="ExternalInput").ap()
    hf = nc.dram_tensor("hf", [P, P], F32, kind="ExternalInput").ap()
    cst = nc.dram_tensor("cst", [P, 8], F32, kind="ExternalInput").ap()
    ccf = nc.dram_tensor("ccf", [P, 8], F32, kind="ExternalInput").ap()
    out = nc.dram_tensor("out", [T, DM], F32, kind="ExternalOutput").ap()

    sp_qf = nc.dram_tensor("sp_qf", [NT, FC, P, TT], F32).ap()
    sp_nm = nc.dram_tensor("sp_nm", [NT, FC, P, DM], F32).ap()
    cc_in = nc.dram_tensor("cc_in", [P, 528], F32).ap()
    cc_out = nc.dram_tensor("cc_out", [NC, P, 528], F32, addr_space="Shared").ap()

    es = ExitStack()
    es.enter_context(nc.allow_low_precision(reason="float32r staging for PE operands"))
    sb = lambda name, shp: es.enter_context(nc.sbuf_tensor(name, shp, F32))
    psx = lambda name: es.enter_context(nc.psum_tensor(name, [P, TT], F32))

    strm = [sb(f"strm{i}", [P, T]) for i in range(FC)]
    wres = sb("wres", [P, 4096])
    big0 = sb("big0", [P, 8448])
    big1 = sb("big1", [P, 4096])
    qft = sb("qft", [P, 4096])
    big3 = sb("big3", [P, 5120])
    sts = sb("sts", [P, 2112])
    stt_ = sb("stt", [P, 2048])
    gsc = sb("gsc", [P, 1152])
    rcs = sb("rcs", [40, 2])
    wblk = sb("wblk", [P, DM])
    wblk2 = sb("wblk2", [P, DM])
    lbt = sb("lbt", [64, DM])
    prjt = sb("prjt", [P, D])
    mskt = sb("mskt", [P, P])
    idnt = sb("idnt", [P, P])
    onet = sb("onet", [P, TT])
    repgt = sb("repgt", [40, 64])
    repmt = sb("repmt", [40, 64])
    cstt = sb("cstt", [P, 8])
    ccft = sb("ccft", [P, 8])
    bb8t = sb("bb8t", [1, DM])
    rbt = sb("rbt", [8, 1])
    hfr = sb("hfr", [P, P])
    rwp_t = sb("rwp_t", [P, 64])
    ra_t = sb("ra_t", [P, 64])

    uA = lambda fc: big0[:, fc * TT:(fc + 1) * TT]
    uB = lambda fc: big0[:, 4096 + fc * TT:4096 + (fc + 1) * TT]
    uC = lambda fc: big1[:, fc * TT:(fc + 1) * TT]
    uD = lambda fc: qft[:, fc * TT:(fc + 1) * TT]
    ktk = lambda s: big0[:, s * 1024:(s + 1) * 1024]
    qk = lambda pr: big0[:, 4096 + pr * TT:4096 + (pr + 1) * TT]
    vtok = lambda s: big0[:, 4096 + s * 1056:4096 + (s + 1) * 1056]
    vsl = lambda s, h: big0[:, 4096 + s * 1056 + h * 66:4096 + s * 1056 + h * 66 + 65]
    vsl66 = lambda s, h: big0[:, 4096 + s * 1056 + h * 66:4096 + s * 1056 + h * 66 + 66]
    tstage = [big0[:, 4096 + i * DM:4096 + (i + 1) * DM] for i in range(2)]
    kfp = lambda pr: big1[:, pr * TT:(pr + 1) * TT]
    qfp = lambda pr: qft[:, pr * TT:(pr + 1) * TT]
    hsl = lambda t_, h: t_[(h % 2) * D:(h % 2) * D + D, (h // 2) * TT:(h // 2 + 1) * TT]
    atm = lambda c0, w: big3[:, c0:c0 + w]
    atm2 = lambda hh, c0, w: big3[:, (0 if hh % 2 == 0 else 3840) + c0:
                                  (0 if hh % 2 == 0 else 3840) + c0 + w]
    scr1 = big3[:, 0:512]
    scr2 = big3[:, 512:1024]
    nst = lambda k: big3[:, 1280 + k * DM:1280 + (k + 1) * DM]
    Gt = big3[0:64, 3328:3840]
    band = lambda i: big3[0:8, 512 + i * 512:1024 + i * 512]
    brow = lambda i: big3[0:1, 512 + i * 512:1024 + i * 512]
    recrow = lambda par: big3[64 * par:64 * par + 1, 0:512]
    sA = sts[:, 0:528]
    sPr = sts[:, 528:1056]
    gB = sts[:, 1056:1584]
    gR = sts[:, 1584:2112]
    ssl = lambda t_, h: t_[(h % 2) * D:(h % 2) * D + D, (h // 2) * 66:(h // 2) * 66 + 65]

    psb = [psx(f"pb{i}") for i in range(8)]

    sems = {}
    S = {}
    for s_ in ("t", "v", "s", "g", "cc", "sm", "iA", "iB", "wb", "wbB", "wr", "wrB",
               "wrC", "wrD", "rl", "rlB", "sq", "snA", "snB", "sc", "soA", "soB"):
        sems[s_] = es.enter_context(nc.semaphore(f"sem_{s_}"))
        S[s_] = 0
    OPS = {k: [] for k in ("tensor", "vector", "scalar", "sync", "gpsimd")}

    def emit(eng, build, wait=(), inc=None, inc_n=1):
        for (sname, val) in wait:
            assert val <= S[sname], f"wait on future {sname}:{val}>{S[sname]}"
        if inc is not None:
            S[inc] += inc_n
        OPS[eng].append((tuple(wait), build, inc, inc_n))
        return S[inc] if inc is not None else None

    W = lambda *pairs: tuple(pairs)
    R = lambda ap: ap.bitcast(F32R)

    def dma(queue, sem, dst, src, wait=()):
        return emit(queue, lambda e, d=dst, s=src: e.dma_start(out=d, in_=s),
                    wait=wait, inc=sem, inc_n=16)

    import traceback as _tb
    MMLOG = getattr(build_nc, "MMLOG", None)
    def mm(lhsT, rhs, outp, start, stop, wait=(), dt=F32R):
        if MMLOG is not None:
            fr = _tb.extract_stack(limit=3)[0]
            MMLOG.append((outp.tensor.name, outp.offset, tuple(map(tuple, outp.ap)),
                          start, stop, fr.lineno))
        if dt is F32R:
            lhsT, rhs = lhsT.bitcast(F32R), rhs.bitcast(F32R)
        return emit("tensor",
                    lambda e, l=lhsT, r=rhs, o=outp, a=start, b=stop:
                    e.matmul(o, l, r, start=a, stop=b), wait=wait, inc="t")

    def tp(outp, inp, n, wait=()):
        if MMLOG is not None:
            MMLOG.append((outp.tensor.name, outp.offset, tuple(map(tuple, outp.ap)),
                          "TP", "TP", 0))
        return emit("tensor", lambda e, o=outp, i=inp, n_=n: e.transpose(o, i, idnt[0:n_, 0:n_]),
                    wait=wait, inc="t")

    def vop(build, wait=()):
        return emit("vector", build, wait=wait, inc="v")

    def sop(build, wait=()):
        return emit("scalar", build, wait=wait, inc="s")

    def gop(build, wait=()):
        return emit("gpsimd", build, wait=wait, inc="g")

    def bounce(eng, sem, wait):
        return emit(eng, lambda e: None, wait=wait, inc=sem)

    for dst, src_, cast in ((lbt, lb, 1), (prjt, prj, 1), (mskt, mskT, 0), (idnt, idn, 0),
                            (onet, onesw, 1), (repgt, repg, 1), (repmt, repm, 1),
                            (cstt, cst, 0), (ccft, ccf, 0), (bb8t, bb8, 1), (rbt, rb, 0),
                            (rwp_t, rwp, 1), (ra_t, ra, 1), (hfr, hf, 1)):
        if cast:
            dma("sync", "sm", R(dst[:]), R(src_[:]))
        else:
            dma("sync", "sm", dst[:], src_[:])
    ld_smalls = S["sm"]
    vop(lambda e: e.tensor_scalar(R(bb8t[:]), bb8t[:], 0.5, 0.0, ALU.mult, ALU.bypass),
        wait=W(("sm", ld_smalls)))
    for k_ in range(2):
        vop(lambda e, k2=k_: e.tensor_scalar(R(nst(k2)[:, TT:2 * TT]), onet[:, 0:TT],
                                             0.0, 0.0, ALU.mult, ALU.bypass))



    def st_row(t_, which):
        base = 32 * t_ if t_ < 3 else 0
        if which in (1, 2):
            c0 = (0 if t_ < 3 else 1024) + (0 if which == 2 else 512)
            return stt_[base:base + 1, c0:c0 + 512]
        c0 = 0 if which == 0 else 512
        return gsc[base:base + 1, c0:c0 + 512]

    def ln_stats_tile(t_, pb=(2, 3), wv=None, ws=None, psfree=0, alt_tr=False):
        tc = slice(t_ * TT, (t_ + 1) * TT)
        base = 32 * t_ if t_ < 3 else 0
        for fc in range(FC):
            xs = big1[:, (fc % 2) * TT:(fc % 2) * TT + TT]
            sqw = W(("t", S["t"]), ("v", S["v"] if wv is None else wv))
            if ws is not None and fc == 0:
                sqw = sqw + (("s", ws),)
            s_ = sop(lambda e, o=xs, i=strm[fc][:, tc]: e.activation(R(o), i, AF.Square),
                     wait=sqw)
            w0 = ()
            if fc == 0:
                w0 = W(("v", max(psfree, S["v"] if wv is None else wv)))
                if ws is not None:
                    w0 = w0 + (("s", ws),)
            mm(onet[:, 0:1], strm[fc][:, tc], psb[pb[0]][0:1, :], fc == 0, fc == FC - 1,
               wait=w0)
            mm(onet[:, 0:1], xs, psb[pb[1]][0:1, :], fc == 0, fc == FC - 1,
               wait=W(("s", s_)))
        tmm = S["t"]
        if alt_tr:
            rm = big1[base:base + 1, 2048:2560]
            re = big1[base:base + 1, 2560:3072]
            rsm, rs = st_row(t_, 1), st_row(t_, 2)
        else:
            rm, rsm, rs, re = (st_row(t_, i) for i in range(4))
        vop(lambda e, o=rm: e.tensor_scalar(R(o), psb[pb[0]][0:1, :], 1.0 / DM, 0.0,
                                            ALU.mult, ALU.bypass), wait=W(("t", tmm)))
        vop(lambda e, o=re: e.tensor_scalar(R(o), psb[pb[1]][0:1, :], 1.0 / DM, 0.0,
                                            ALU.mult, ALU.bypass))
        b1 = bounce("scalar", "s", W(("v", S["v"])))
        vop(lambda e, o=rs, a=rm: e.tensor_tensor(R(o), a, a, ALU.mult),
            wait=W(("s", b1)))
        b2x = bounce("scalar", "s", W(("v", S["v"])))
        vop(lambda e, o=re, b_=rs: e.tensor_tensor(R(o), o, b_, ALU.subtract),
            wait=W(("s", b2x)))
        vv = S["v"]
        ss = sop(lambda e, o=rs, i=re, b2=base: e.activation(
            R(o), i, AF.Sqrt, bias=cstt[b2:b2 + 1, C_EPS_LN:C_EPS_LN + 1]), wait=W(("v", vv)))
        vop(lambda e, o=rs: e.reciprocal(R(o), o), wait=W(("s", ss)))
        b3 = bounce("scalar", "s", W(("v", S["v"])))
        vop(lambda e, o=rsm, a=rm, b_=rs: e.tensor_tensor(R(o), a, b_, ALU.mult),
            wait=W(("s", b3)))
        return S["v"]

    def ln_stats(tile_waits=None):
        out = []
        psfree = 0
        for t_ in range(NT):
            twv = tile_waits[t_] if tile_waits is not None else None
            out.append(ln_stats_tile(t_, wv=twv, psfree=psfree))
            psfree = out[-1]
        return out

    def make_u(t_, stats_v, dst, vguard=None, extra=(), row=None):
        row = onet if row is None else row
        tc = slice(t_ * TT, (t_ + 1) * TT)
        ob = 32 * t_ if t_ < 3 else 0
        if vguard is None:
            vguard = S["v"]
        mm(row[ob:ob + 1, 0:P], st_row(t_, 2), psb[4][:, :], True, True,
           wait=W(("v", max(stats_v, vguard)), ("t", S["t"])) + tuple(extra))
        bt = mm(row[ob:ob + 1, 0:P], st_row(t_, 1), psb[5][:, :], True, True)
        for fc in range(FC):
            vop(lambda e, o=dst(fc), i=strm[fc][:, tc]:
                e.tensor_tensor(R(o), i, psb[4][:, :], ALU.mult),
                wait=W(("t", bt)) if fc == 0 else ())
        bu_ = bounce("scalar", "s", W(("v", S["v"])))
        for fc in range(FC):
            vop(lambda e, o=dst(fc): e.tensor_tensor(R(o), o, psb[5][:, :], ALU.subtract),
                wait=W(("s", bu_)) if fc == 0 else ())
        return S["v"]

    GS = lambda k: gsc[:, k * P:(k + 1) * P]
    S_G, S_U, S_Q, S_R, S_ACC, S_B, S_F, S_P, S_B0 = range(9)
    bnd = lambda i, r: big3[r:r + 8, 512 + i * 512:1024 + i * 512]

    def chain(steps, init_wait=None):
        prev = init_wait
        for forced, build in steps:
            eng = forced if forced else "vector"
            if prev is not None and prev[0] == {"vector": "v", "gpsimd": "g",
                                                "scalar": "s"}[eng]:
                ob = "scalar" if eng != "scalar" else "vector"
                bsem = {"vector": "v", "gpsimd": "g", "scalar": "s"}[ob]
                prev = (bsem, bounce(ob, bsem, W(prev)))
            sem = {"vector": "v", "gpsimd": "g", "scalar": "s"}[eng]
            val = emit(eng, build, wait=W(prev) if prev else (), inc=sem)
            prev = (sem, val)
        return prev

    def colsums():
        for fc in range(FC):
            mm(rwp_t[:, fc * 8:fc * 8 + 8], onet[:, 0:1], psb[2][0:8, 0:1],
               fc == 0, fc == FC - 1, dt=F32,
               wait=W(("sm", ld_smalls)) if fc == 0 else ())
            mm(ra_t[:, fc * 8:fc * 8 + 8], onet[:, 0:1], psb[3][0:8, 0:1],
               fc == 0, fc == FC - 1, dt=F32)
        tmm = S["t"]
        vop(lambda e: e.tensor_copy(rcs[0:8, 0:1], psb[2][0:8, 0:1]), wait=W(("t", tmm)))
        vop(lambda e: e.tensor_scalar(rcs[0:8, 1:2], psb[3][0:8, 0:1], -1.0, 0.0,
                                      ALU.mult, ALU.bypass))
        vop(lambda e: e.tensor_copy(rcs[32:40, 0:1], psb[2][0:8, 0:1]))
        vop(lambda e: e.tensor_scalar(rcs[32:40, 1:2], psb[3][0:8, 0:1], -1.0, 0.0,
                                      ALU.mult, ALU.bypass))

    def gating_setup_tile(t_, wv, extra=(), banks=(2, 3, 6, 7)):
        if True:
            tc = slice(t_ * TT, (t_ + 1) * TT)
            lb_, mb_, sb_, smb_ = (psb[b_] for b_ in banks)
            for fc in range(FC):
                w0 = (W(("v", wv)) + tuple(extra)) if fc == 0 else ()
                mm(rwp_t[:, fc * 8:fc * 8 + 8], strm[fc][:, tc], lb_[0:8, :],
                   fc == 0, fc == FC - 1, wait=w0)
            for fc in range(FC):
                mm(ra_t[:, fc * 8:fc * 8 + 8], strm[fc][:, tc], mb_[0:8, :],
                   fc == 0, fc == FC - 1)
            ob2 = 32 * t_ if t_ < 3 else 0
            mm(onet[ob2:ob2 + 1, 0:8], st_row(t_, 2), sb_[0:8, :], True, True)
            bt = mm(onet[ob2:ob2 + 1, 0:8], st_row(t_, 1), smb_[0:8, :], True, True)
            vop(lambda e: e.tensor_copy(R(bnd(0, 0)), lb_[0:8, :]), wait=W(("t", bt)))
            b0x = bounce("scalar", "s", W(("v", S["v"])))
            vop(lambda e: e.tensor_tensor(R(bnd(0, 0)), bnd(0, 0), sb_[0:8, :], ALU.mult),
                wait=W(("s", b0x)))
            b1 = bounce("scalar", "s", W(("v", S["v"])))
            vop(lambda e: e.scalar_tensor_tensor(R(bnd(0, 0)), smb_[0:8, :],
                rcs[0:8, 0:1], bnd(0, 0), ALU.mult, ALU.subtract), wait=W(("s", b1)))
            vv = S["v"]
            ss = sop(lambda e: e.activation(R(bnd(0, 0)), bnd(0, 0), AF.Exp,
                                            scale=-1.0, bias=rbt[0:8, 0:1]),
                     wait=W(("v", vv)))
            mv = vop(lambda e, t2=t_: e.tensor_copy(R(bnd(1 + t2 // 2, (t2 % 2) * 32)),
                                                    mb_[0:8, :]), wait=W(("s", ss)))
            b1x = bounce("scalar", "s", W(("v", S["v"])))
            vop(lambda e, t2=t_: e.tensor_tensor(
                R(bnd(1 + t2 // 2, (t2 % 2) * 32)), bnd(1 + t2 // 2, (t2 % 2) * 32),
                sb_[0:8, :], ALU.mult), wait=W(("s", b1x)))
            b2 = bounce("scalar", "s", W(("v", S["v"])))
            vop(lambda e, t2=t_: e.scalar_tensor_tensor(
                R(bnd(1 + t2 // 2, (t2 % 2) * 32)), smb_[0:8, :],
                rcs[(t2 % 2) * 32:(t2 % 2) * 32 + 8, 1:2],
                bnd(1 + t2 // 2, (t2 % 2) * 32), ALU.mult, ALU.add), wait=W(("s", b2)))
            smm = mm(onet[0:8, 0:1], bnd(0, 0), lb_[0:1, :], True, True,
                     wait=W(("s", ss), ("v", S["v"])))
            vop(lambda e: e.reciprocal(R(bnd(3, 0)[0:1, :]), lb_[0:1, :]),
                wait=W(("t", smm)))
            vv = S["v"]
            bmm = mm(onet[0:1, 0:8], bnd(3, 0)[0:1, :], lb_[0:8, :], True, True,
                     wait=W(("v", vv)))
            vop(lambda e: e.tensor_tensor(R(bnd(0, 0)), bnd(0, 0), lb_[0:8, :],
                                          ALU.mult), wait=W(("t", bmm)))
            for k in range(4):
                bank = psb[4 + k % 2]
                tp_ = tp(bank[:, 0:8], bnd(0, 0)[0:8, k * P:(k + 1) * P], 8,
                         wait=W(("v", S["v"])))
                vop(lambda e, b_=bank, c=t_ * 32 + k * 8: e.tensor_copy(
                    GS(S_G)[:, c:c + 8], b_[:, 0:8]), wait=W(("t", tp_)))

    def gating_chain():
        ts_ = lambda o, i, s1, s2, o1, o2: (lambda e: e.tensor_scalar(
            GS(o), GS(i), s1, s2, o1, o2))
        tt_ = lambda o, a, b: (lambda e: e.tensor_tensor(GS(o), GS(a), GS(b), ALU.mult))
        steps = []
        steps.append(("scalar", lambda e: e.activation(GS(S_U), GS(S_G), AF.Ln)))
        steps.append((None, ts_(S_U, S_U, -INV_LN2, 0.0, ALU.mult, ALU.bypass)))
        steps.append((None, ts_(S_Q, S_U, 0.0, 2.0, ALU.mult, ALU.add)))
        for v in (32, 16, 8, 4, 2, 1):
            steps.append(("vector", ts_(S_B, S_U, float(v), 0.0, ALU.is_ge, ALU.bypass)))
            steps.append((None, lambda e, v2=float(v): e.scalar_tensor_tensor(
                GS(S_U), GS(S_B), -v2, GS(S_U), ALU.mult, ALU.add)))
            steps.append((None, ts_(S_F, S_B, float(2 ** v - 1), 1.0, ALU.mult, ALU.add)))
            steps.append((None, tt_(S_Q, S_Q, S_F)))
        steps.append((None, lambda e: e.scalar_tensor_tensor(
            GS(S_R), GS(S_G), 128.0, GS(S_Q), ALU.mult, ALU.mult)))
        steps.append(("vector", ts_(S_B, S_R, 128.0, 0.0, ALU.is_lt, ALU.bypass)))
        steps.append((None, ts_(S_P, S_B, -0.5, 1.0, ALU.mult, ALU.add)))
        steps.append((None, ts_(S_F, S_B, 1.0, 1.0, ALU.mult, ALU.add)))
        steps.append((None, tt_(S_R, S_R, S_F)))
        steps.append(("vector", ts_(S_B, S_R, 256.0, 0.0, ALU.is_ge, ALU.bypass)))
        steps.append((None, ts_(S_F, S_B, -0.5, 1.0, ALU.mult, ALU.add)))
        steps.append((None, tt_(S_R, S_R, S_F)))
        steps.append((None, ts_(S_F, S_B, 1.0, 1.0, ALU.mult, ALU.add)))
        steps.append((None, tt_(S_P, S_P, S_F)))
        steps.append((None, ts_(S_R, S_R, -128.0, 0.0, ALU.add, ALU.bypass)))
        steps.append((None, ts_(S_ACC, S_R, 0.0, 0.0, ALU.mult, ALU.bypass)))
        for v in (64, 32, 16, 8, 4, 2, 1):
            steps.append(("vector", ts_(S_B, S_R, float(v), 0.0, ALU.is_ge, ALU.bypass)))
            steps.append((None, lambda e, v2=float(v): e.scalar_tensor_tensor(
                GS(S_R), GS(S_B), -v2, GS(S_R), ALU.mult, ALU.add)))
            steps.append((None, lambda e, v2=float(v): e.scalar_tensor_tensor(
                GS(S_ACC), GS(S_B), v2, GS(S_ACC), ALU.mult, ALU.add)))
        steps.append((None, ts_(S_B0, S_B, 1.0, 0.0, ALU.mult, ALU.bypass)))
        steps.append(("vector", ts_(S_F, S_R, 0.5 + 2.0 ** -17, 0.0, ALU.is_ge, ALU.bypass)))
        steps.append(("vector", ts_(S_U, S_R, 0.5 - 2.0 ** -17, 0.0, ALU.is_ge, ALU.bypass)))
        steps.append((None, lambda e: e.scalar_tensor_tensor(
            GS(S_U), GS(S_F), -1.0, GS(S_U), ALU.mult, ALU.add)))
        steps.append((None, tt_(S_U, S_U, S_B0)))
        steps.append((None, lambda e: e.scalar_tensor_tensor(
            GS(S_F), GS(S_F), 1.0, GS(S_U), ALU.mult, ALU.add)))
        steps.append((None, ts_(S_ACC, S_ACC, 128.0, 0.0, ALU.add, ALU.bypass)))
        steps.append((None, lambda e: e.scalar_tensor_tensor(
            GS(S_ACC), GS(S_ACC), 1.0, GS(S_F), ALU.mult, ALU.add)))
        steps.append(("vector", lambda e: e.reciprocal(GS(S_Q), GS(S_Q))))
        steps.append((None, lambda e: e.scalar_tensor_tensor(
            GS(S_ACC), GS(S_ACC), 0.0078125, GS(S_Q), ALU.mult, ALU.mult)))
        steps.append((None, tt_(S_ACC, S_ACC, S_P)))
        prev = chain(steps, init_wait=("v", S["v"]))
        return prev

    def gating_finish(prev):
        for t_ in range(NT):
            for k in range(4):
                bank = psb[4 + k % 2]
                w0 = W(prev, ("v", S["v"])) if (t_ == 0 and k == 0) else W(("v", S["v"]))
                tp_ = tp(bank[0:8, 0:P],
                         GS(S_ACC)[:, t_ * 32 + k * 8:t_ * 32 + k * 8 + 8], P, wait=w0)
                vop(lambda e, b_=bank, t2=t_, k2=k: e.tensor_copy(
                    R(bnd(3 + t2 // 2, (t2 % 2) * 32)[0:8, k2 * P:(k2 + 1) * P]),
                    b_[0:8, 0:P]), wait=W(("t", tp_)))
        return S["v"]

    def G_for(t_):
        vv = S["v"]
        mm(repgt[(t_ % 2) * 32:(t_ % 2) * 32 + 8, :], bnd(3 + t_ // 2, (t_ % 2) * 32),
           psb[6][0:64, :], True, True, wait=W(("v", vv)))
        m2 = mm(repmt[(t_ % 2) * 32:(t_ % 2) * 32 + 8, :], bnd(1 + t_ // 2, (t_ % 2) * 32),
                psb[7][0:64, :], True, True)
        vop(lambda e: e.tensor_copy(R(Gt), psb[7][0:64, :]), wait=W(("t", m2)))
        bG = bounce("scalar", "s", W(("v", S["v"])))
        vop(lambda e: e.tensor_tensor(R(Gt), Gt, psb[6][0:64, :], ALU.mult),
            wait=W(("s", bG)))
        return S["v"]

    def load_whalf(src, half, wait=()):
        last = None
        for fc in range(FC):
            last = dma("sync", "wr", R(wres[:, fc * TT:(fc + 1) * TT]),
                       R(src[fc * P:(fc + 1) * P, half * TT:(half + 1) * TT]),
                       wait=wait if fc == 0 else ())
        return ("wr", last)

    WRS = ("wr", "wrB", "wrC", "wrD")

    def load_w_grouped(src, half, wfree):
        tg = []
        for g in range(4):
            for k in range(2):
                fc = 2 * g + k
                dma("sync", WRS[g], R(wres[:, fc * TT:(fc + 1) * TT]),
                    R(src[fc * P:(fc + 1) * P, half * TT:(half + 1) * TT]),
                    wait=W(("t", wfree[fc])))
            tg.append((WRS[g], S[WRS[g]]))
        return tg

    def ffn(stats_tiles, s_guard=None, setup_ext=False):
        wfree = [S["t"]] * FC
        s_entry = S["s"] if s_guard is None else s_guard
        u4 = (uA, uB, uC, uD)
        uvs = [make_u(t_, stats_tiles[t_], u4[t_], row=hfr) for t_ in range(NT)]
        ev01 = S["v"]
        if not setup_ext:
            psg = S["v"]
            for t_ in range(NT):
                gating_setup_tile(t_, max(stats_tiles[t_], psg))
        evac_v = [ev01, ev01, S["v"], S["v"]]
        chain_prev = gating_chain()

        def lora_pass(ocs):
            bfree = [None, None]
            tile_done = [0] * NT
            for t_ in range(NT):
                gv = G_for(t_)
                tc = slice(t_ * TT, (t_ + 1) * TT)
                for i, oc in enumerate(ocs):
                    bank = psb[4 + i % 2]
                    w0 = W(("v", gv)) if i == 0 else ()
                    if bfree[i % 2] is not None and i >= 2:
                        w0 = w0 + (("v", bfree[i % 2]),)
                    lmm = mm(lbt[:, oc * P:(oc + 1) * P], Gt, bank[:, :], True, True,
                             wait=w0)
                    bfree[i % 2] = vop(lambda e, o=strm[oc][:, tc], b_=bank:
                        e.tensor_tensor(R(o), o, b_[:, :], ALU.add),
                        wait=W(("t", lmm)))
                tile_done[t_] = S["v"]
            return tile_done

        for half in range(2):
            tg = load_w_grouped(bwT, half, wfree)
            for t_ in range(NT):
                ufn = u4[t_]
                tc = slice(t_ * TT, (t_ + 1) * TT)
                for fc in range(FC):
                    for oc_ in range(4):
                        w0 = ()
                        if t_ == 0 and oc_ == 0 and fc % 2 == 0:
                            w0 = W(tg[fc // 2])
                        if fc == 0:
                            w0 = w0 + (("v", max(uvs[t_], evac_v[oc_])),)
                            if half == 0 and t_ == 0:
                                w0 = w0 + (("s", s_entry),)
                        mm(wres[:, fc * TT + oc_ * P:fc * TT + oc_ * P + P],
                           ufn(fc), psb[oc_][:, :], fc == 0, False, wait=w0)
                    if t_ == NT - 1:
                        wfree[fc] = S["t"]
                for oc_ in range(4):
                    oc = half * 4 + oc_
                    lmm = mm(bb8t[0:1, oc * P:(oc + 1) * P], onet[0:1, 0:TT],
                             psb[oc_][:, :], False, True)
                    evac_v[oc_] = vop(
                        lambda e, o=strm[oc][:, tc], b_=psb[oc_]:
                        e.tensor_tensor(R(o), o, b_[:, :], ALU.add),
                        wait=W(("t", lmm)))
        gating_finish(chain_prev)
        return lora_pass(tuple(range(2 * NT)))

    colsums()
    stage_s = [0, 0]
    ibank_s = [0, 0, 0, 0]
    sv1_tiles = [0] * NT
    in_tile_s = [0] * NT
    in_psg = [0]
    for it in range(16):
        stg = tstage[it % 2]
        isem = "iA" if it % 2 == 0 else "iB"
        ld = dma("sync", isem, R(stg), R(xin[it * P:(it + 1) * P, :]),
                 wait=W(("s", stage_s[it % 2])))
        for fc in range(FC):
            bi = fc % 4
            bank = psb[bi]
            w0 = W((isem, ld), ("s", ibank_s[bi]))
            if it == 0 and fc == 0:
                w0 = w0 + (("sm", ld_smalls), ("v", S["v"]))
            t_ = tp(bank[:, 0:P], stg[:, fc * P:(fc + 1) * P], P, wait=w0)
            sop(lambda e, b_=bank, d=strm[fc][:, it * P:(it + 1) * P]:
                e.activation(R(d), b_[:, 0:P], AF.Copy), wait=W(("t", t_)))
            ibank_s[bi] = S["s"]
        stage_s[it % 2] = S["s"]
        if it % 4 == 3:
            t2 = it // 4
            tin_s = S["s"]
            sv1_tiles[t2] = ln_stats_tile(t2, pb=(4, 5), ws=tin_s,
                                          psfree=in_psg[0], alt_tr=True)
            gating_setup_tile(t2, sv1_tiles[t2], extra=(("s", tin_s),),
                              banks=(4, 5, 6, 7))
            in_psg[0] = S["v"]
    input_v = S["v"]
    input_s = S["s"]
    td1 = ffn(sv1_tiles, s_guard=input_s, setup_ext=True)

    sv2 = ln_stats(td1)
    qf_spill_st = 0
    nm_spill_st = [0, 0]
    prev_tail_v = None
    prev_tail_s = None
    for j in range(NT):
        uv = make_u(j, sv2[j], uA, vguard=prev_tail_v, extra=(
            (("s", prev_tail_s),) if prev_tail_s else ()))

        def proj_qk(wsrc, w0extra, dst=None):
            if dst is None:
                dst = qk
            used_t = [S["t"], S["t"]]
            for fc in range(FC):
                wb = wblk if fc % 2 == 0 else wblk2
                ws = "wb" if fc % 2 == 0 else "wbB"
                qn = "sync" if fc % 2 == 0 else "gpsimd"
                ld = dma(qn, ws, R(wb[:]), R(wsrc[fc * P:(fc + 1) * P, :]),
                         wait=W(("t", used_t[fc % 2])))
                for oc in range(FC):
                    w0 = W((ws, ld)) if oc == 0 else ()
                    if fc == 0 and oc == 0:
                        w0 = w0 + tuple(w0extra)
                    mm(wb[:, oc * P:(oc + 1) * P], uA(fc), psb[oc][:, :],
                       fc == 0, fc == FC - 1, wait=w0)
                used_t[fc % 2] = S["t"]
            tmm = S["t"]
            fv = fs = True
            for oc in range(FC):
                if oc % 2 == 0:
                    vop(lambda e, o=dst(oc), b_=psb[oc]: e.tensor_copy(R(o), b_[:, :]),
                        wait=W(("t", tmm)) if fv else ())
                    fv = False
                else:
                    sop(lambda e, o=dst(oc), b_=psb[oc]:
                        e.activation(R(o), b_[:, :], AF.Copy),
                        wait=W(("t", tmm)) if fs else ())
                    fs = False
            return S["v"], S["s"]

        def fm(dst_pair, src_v, extra_wait=(), src=None, filler=None):
            sfn = qk if src is None else src
            bank_done = [src_v, src_v]
            for pr in range(FC):
                if filler is not None:
                    filler(pr)
                bs = 4 if pr % 2 == 0 else 6
                sc_ = scr1 if pr % 2 == 0 else scr2
                w0 = W(("v", max(src_v, bank_done[pr % 2])))
                if pr == 0:
                    w0 = w0 + tuple(extra_wait)
                mm(prjt[0:D, :], sfn(pr)[0:D, :], psb[bs][0:D, :], True, True, wait=w0)
                m2 = mm(prjt[D:P, :], sfn(pr)[D:P, :], psb[bs + 1][0:D, :], True, True)
                vop(lambda e, b2=bs, s2=sc_: e.tensor_scalar(
                    R(s2[0:D, :]), psb[b2][0:D, :], 0.0, 0.0, ALU.min, ALU.bypass),
                    wait=W(("t", m2), ("s", S["s"])))
                vop(lambda e, b2=bs, s2=sc_: e.tensor_scalar(
                    R(s2[D:P, :]), psb[b2 + 1][0:D, :], 0.0, 0.0, ALU.min, ALU.bypass))
                vv = S["v"]
                ssx = sop(lambda e, s2=sc_: e.activation(R(s2), s2, AF.Exp),
                          wait=W(("v", vv)))
                vop(lambda e, b2=bs, s2=sc_, o=dst_pair(pr): e.scalar_tensor_tensor(
                    R(o[0:D, :]), psb[b2][0:D, :], 0.0, s2[0:D, :], ALU.max, ALU.add),
                    wait=W(("s", ssx)))
                vop(lambda e, b2=bs, s2=sc_, o=dst_pair(pr): e.scalar_tensor_tensor(
                    R(o[D:P, :]), psb[b2 + 1][0:D, :], 0.0, s2[D:P, :], ALU.max, ALU.add))
                bank_done[pr % 2] = S["v"]
            return S["v"]

        qv, qs_ = proj_qk(wq, (("v", uv),))
        kv_, ks_ = proj_qk(wk, (("v", qv), ("s", qs_)), dst=kfp)

        def vproj_wave_evacs(w_, stop_all):
            for ti in range(2):
                ts = 2 * w_ + ti
                for pc in range(2):
                    def ev(e, t2=ts, p2=pc, b2=ti * 2 + pc):
                        src = psb[b2][:, :].rearrange("p (h d) -> p h d", h=8)
                        dst = vtok(t2)[:, p2 * 528:(p2 + 1) * 528].rearrange(
                            "p (h s) -> p h s", h=8)[:, :, 0:64]
                        return e.tensor_copy(R(dst), src)
                    vop(ev, wait=W(("t", stop_all)) if (ti == 0 and pc == 0) else ())
                def ones_ev(e, t2=ts):
                    dst = vtok(t2)[:, :].rearrange("p (h s) -> p h s", h=16)[:, :, 64:66]
                    src = onet[:, 0:32].rearrange("p (h s) -> p h s", h=16)
                    return e.tensor_scalar(R(dst), src, 0.0, 1.0, ALU.mult, ALU.add)
                vop(ones_ev)

        def vproj_filler(w_):
            used_t = [S["t"], S["t"]]

            def filler(fc):
                wb = wblk if fc % 2 == 0 else wblk2
                ws = "wb" if fc % 2 == 0 else "wbB"
                qn = "sync" if fc % 2 == 0 else "gpsimd"
                ld = dma(qn, ws, R(wb[:]), R(wv[fc * P:(fc + 1) * P, :]),
                         wait=W(("t", used_t[fc % 2])))
                for ti in range(2):
                    ts = 2 * w_ + ti
                    for pc in range(2):
                        w0 = W((ws, ld)) if (ti == 0 and pc == 0) else ()
                        if fc == 0 and ti == 0 and pc == 0:
                            w0 = w0 + (("v", S["v"]), ("s", S["s"]))
                        mm(uA(fc)[:, ts * P:(ts + 1) * P],
                           wb[:, pc * TT:(pc + 1) * TT],
                           psb[ti * 2 + pc][:, :], fc == 0, fc == FC - 1, wait=w0)
                used_t[fc % 2] = S["t"]
            return filler

        qfv = fm(qfp, kv_, extra_wait=W(("sq", qf_spill_st), ("s", ks_)),
                 filler=vproj_filler(0))
        stopA = S["t"]
        vproj_wave_evacs(0, stopA)
        for fcq in range(FC):
            qf_spill_st = dma("gpsimd", "sq", R(sp_qf[j, fcq]), R(qfp(fcq)),
                              wait=W(("v", qfv)))
        kfv = fm(kfp, qfv, src=kfp, extra_wait=W(("s", ks_)),
                 filler=vproj_filler(1))
        stopB = S["t"]
        vproj_wave_evacs(1, stopB)
        vtok_v = S["v"]

        tb_done = [("v", S["v"]), ("v", S["v"]), ("v", S["v"]), ("s", S["s"])]
        for s_ in range(4):
            for pr in range(FC):
                bi = (s_ * FC + pr) % 4
                bank = psb[2 + bi]
                t_ = tp(bank[:, 0:P], kfp(pr)[:, s_ * P:(s_ + 1) * P], P,
                        wait=W(tb_done[bi]))
                if bi < 2:
                    vop(lambda e, b_=bank, d=ktk(s_)[:, pr * P:(pr + 1) * P]:
                        e.tensor_copy(R(d), b_[:, 0:P]), wait=W(("t", t_)))
                    tb_done[bi] = ("v", S["v"])
                else:
                    sop(lambda e, b_=bank, d=ktk(s_)[:, pr * P:(pr + 1) * P]:
                        e.activation(R(d), b_[:, 0:P], AF.Copy), wait=W(("t", t_)))
                    tb_done[bi] = ("s", S["s"])
        ktk_v = S["v"]
        ktk_s = S["s"]

        def state_filler(h):
            bi = (2 + h // 8) if h % 2 == 0 else (4 + h // 8)
            cc0 = ((h % 8) // 2) * 66
            for s_ in range(4):
                first = (h in (0, 1, 8, 9)) and s_ == 0
                last = (h in (6, 7, 14, 15)) and s_ == 3
                w0 = W(("v", max(vtok_v, ktk_v)), ("s", ktk_s)) \
                    if h in (0, 1) and s_ == 0 else ()
                mm(ktk(s_)[:, h * D:(h + 1) * D], vsl66(s_, h),
                   psb[bi][0:D, cc0:cc0 + 66], first, last, wait=w0)

        abank_free = [(S["v"], S["s"])] * 2
        ndbank_free = [S["s"]] * 2
        atm_done = [(0, 0)] * H
        nd_mm_done = [0] * H

        def emit_A(hh):
            for s_ in range(4):
                bank = psb[s_ % 2]
                av, asv = abank_free[s_ % 2]
                a_ = mm(kfp(hh // 2)[(hh % 2) * D:(hh % 2) * D + D, s_ * P:(s_ + 1) * P],
                        hsl(qft, hh)[:, s_ * P:TT], bank[:, 0:AW[s_]], True, True,
                        wait=W(("v", av), ("s", asv)))
                mv_ = vop(lambda e, b_=bank, hh2=hh, c0=AOFF[s_]: e.tensor_tensor(
                    R(atm2(hh2, c0, P)), b_[:, 0:P], mskt[:, :], ALU.mult),
                    wait=W(("t", a_)))
                if AW[s_] > P:
                    sop(lambda e, b_=bank, hh2=hh, c0=AOFF[s_], wd=AW[s_]: e.activation(
                        R(atm2(hh2, c0 + P, wd - P)), b_[:, P:wd], AF.Copy),
                        wait=W(("v", mv_)))
                abank_free[s_ % 2] = (S["v"], S["s"])
            atm_done[hh] = (S["v"], S["s"])

        def emit_nd(hh):
            nb = psb[6 + hh % 2]
            for s_ in range(4):
                if s_ == 0:
                    w0 = W(("v", atm_done[hh][0]), ("s", atm_done[hh][1]),
                           ("s", max(ndbank_free[hh % 2], atm_done[hh][1])))
                    w0 = tuple({k: v for k, v in w0}.items())
                else:
                    w0 = ()
                stop = (j == 0 and s_ == 3)
                mm(vsl(s_, hh), atm2(hh, AOFF[s_], AW[s_]), nb[0:65, s_ * P:TT],
                   s_ == 0, stop, wait=w0)
            if j > 0:
                mm(ssl(sPr, hh), hsl(qft, hh), nb[0:65, :], False, True)
            nd_mm_done[hh] = S["t"]
            pr = hh // 2
            sop(lambda e, b_=nb, hh2=hh: e.activation(
                R(nst(hh2 // 2 % 2)[(hh2 % 2) * D:(hh2 % 2) * D + D, 0:TT]),
                b_[0:D, :], AF.Copy),
                wait=W(("t", nd_mm_done[hh]),
                       ("snA" if pr % 2 == 0 else "snB", nm_spill_st[pr % 2])))
            sop(lambda e, b_=nb, hh2=hh: e.activation(
                R(nst(hh2 // 2 % 2)[(hh2 % 2) * D:(hh2 % 2) * D + 1, TT:2 * TT]),
                b_[D:65, :], AF.Copy))
            ndbank_free[hh % 2] = S["s"]
            if hh % 2 == 1:
                nm_spill_st[pr % 2] = dma(
                    "gpsimd", "snA" if pr % 2 == 0 else "snB", R(sp_nm[j, pr]),
                    R(nst(pr % 2)[:, :]), wait=W(("s", S["s"])))

        pre_nd_t = S["t"]
        stj_box = [0]
        emit_A(0)
        for h in range(H):
            if h + 1 < H:
                emit_A(h + 1)
            state_filler(h)
            if h == H - 1:
                tmm = S["t"]
                vop(lambda e: e.tensor_copy(R(sA[0:D, 0:264]), psb[2][0:D, 0:264]),
                    wait=W(("t", tmm), ("v", S["v"])))
                vop(lambda e: e.tensor_copy(R(sA[D:P, 0:264]), psb[4][0:D, 0:264]))
                vop(lambda e: e.tensor_copy(R(sA[0:D, 264:528]), psb[3][0:D, 0:264]))
                vop(lambda e: e.tensor_copy(R(sA[D:P, 264:528]), psb[5][0:D, 0:264]))
                stj_box[0] = S["v"]
            emit_nd(h)
        stj_v = stj_box[0]
        prev_tail_v = S["v"]
        prev_tail_s = max(ndbank_free)

        if j == 0:
            vop(lambda e: e.tensor_copy(R(sPr[:, :]), sA[:, :]),
                wait=W(("v", stj_v), ("t", S["t"])))
        elif j < NT - 1:
            vop(lambda e: e.tensor_tensor(R(sPr[:, :]), sPr[:, :], sA[:, :], ALU.add),
                wait=W(("v", stj_v), ("t", S["t"])))
        else:
            total_v = vop(lambda e: e.tensor_tensor(R(gR[:, :]), sPr[:, :], sA[:, :],
                                                    ALU.add),
                          wait=W(("v", stj_v), ("t", pre_nd_t)))
    total = gR

    dma("gpsimd", "sc", R(cc_in[:, :]), R(total[:, :]), wait=W(("v", total_v)))
    st_now = S["sc"]
    emit("gpsimd", lambda e: e.collective_compute(
        "AllGather", ALU.bypass, replica_groups=[list(range(NC))],
        ins=[R(cc_in[:, :])], outs=[R(cc_out[:, :, :])]), wait=W(("sc", st_now)), inc="cc")
    cc_val = S["cc"]
    p1_end_t = S["t"]
    p1_end_s = S["s"]
    slg = lambda c: big0[:, 4096 + c * 528:4096 + (c + 1) * 528]
    for c_ in range(NC):
        w0 = W(("cc", cc_val), ("t", p1_end_t), ("s", p1_end_s)) if c_ == 0 \
            else W(("cc", cc_val))
        dma("gpsimd", "sc", R(slg(c_)), R(cc_out[c_, :, :]), wait=w0)
    dz = S["sc"]
    for c_ in range(NC):
        vop(lambda e, c2=c_: e.tensor_scalar(R(slg(c2)), slg(c2), ccft[:, c2:c2 + 1],
                                             0.0, ALU.mult, ALU.bypass),
            wait=W(("sc", dz)) if c_ == 0 else ())
    b1 = bounce("scalar", "s", W(("v", S["v"])))
    for p_ in range(4):
        vop(lambda e, p2=p_: e.tensor_tensor(R(slg(2 * p2)), slg(2 * p2),
                                             slg(2 * p2 + 1), ALU.add),
            wait=W(("s", b1)) if p_ == 0 else ())
    b2 = bounce("scalar", "s", W(("v", S["v"])))
    for p_ in range(2):
        vop(lambda e, p2=p_: e.tensor_tensor(R(slg(4 * p2)), slg(4 * p2),
                                             slg(4 * p2 + 2), ALU.add),
            wait=W(("s", b2)) if p_ == 0 else ())
    b3 = bounce("scalar", "s", W(("v", S["v"])))
    vop(lambda e: e.tensor_tensor(R(gB[:, :]), slg(0), slg(4), ALU.add),
        wait=W(("s", b3)))
    gb_v = S["v"]

    p2_tail_t = S["t"]
    p2_tail_v = S["v"]
    wfree_wo = [p1_end_t] * FC
    wo_evac = [S["v"]] * 4
    p2_entry_s = S["s"]
    rl_carry = [0]
    for j in range(NT):
        def nmr(pr):
            return big1[:, pr * DM:(pr + 1) * DM] if pr < 4 else \
                   big0[:, (pr - 4) * DM:(pr - 3) * DM]
        if j == 0:
            w0 = W(("t", p1_end_t), ("v", total_v), ("sq", S["sq"]),
                   ("snA", S["snA"]), ("snB", S["snB"]))
            for f2 in range(FC):
                dma("sync", "rl", R(qfp(f2)), R(sp_qf[j, f2]),
                    wait=w0 if f2 == 0 else ())
            for p2 in range(FC):
                dma("sync", "rl", R(nmr(p2)), R(sp_nm[j, p2]))
            rl_all = ("rl", S["rl"])
        else:
            rl_all = ("rlB", rl_carry[0])
        ndfree = [gb_v, gb_v]
        bkfree = [gb_v, gb_v]
        attp = [0] * FC
        denv = 0
        for h in range(H):
            nb = psb[h % 2]
            pr = h // 2
            par = h % 2
            gmm = mm(ssl(gB, h), hsl(qft, h), nb[0:65, :], True, True,
                     wait=W(rl_all, ("v", max(gb_v, ndfree[h % 2]))))
            vop(lambda e, b_=nb, p2=pr, pa=par: e.scalar_tensor_tensor(
                R(recrow(pa)), b_[D:65, :], 1e-6,
                nmr(p2)[pa * D:pa * D + 1, TT:2 * TT], ALU.add, ALU.add),
                wait=W(("t", gmm)))
            denv = S["v"]
            vop(lambda e, b_=nb, p2=pr, pa=par: e.tensor_tensor(
                R(qk(p2)[pa * D:pa * D + D, :]), b_[0:D, :],
                nmr(p2)[pa * D:pa * D + D, 0:TT], ALU.add))
            ndfree[h % 2] = S["v"]
            if par == 1:
                bk = psb[2 + pr % 2]
                mm(onet[0:1, 0:D], recrow(0), bk[0:D, :], True, True,
                   wait=W(("v", max(denv, bkfree[pr % 2]))))
                bmm = mm(onet[64:65, 0:D], recrow(1), bk[D:P, :], True, True, dt=F32)
                vop(lambda e, b2=bk: e.reciprocal(R(scr2), b2[:, :]),
                    wait=W(("t", bmm)))
                bp = bounce("scalar", "s", W(("v", S["v"])))
                vop(lambda e, p2=pr: e.tensor_tensor(
                    R(qk(p2)), qk(p2), scr2, ALU.mult), wait=W(("s", bp)))
                bkfree[pr % 2] = S["v"]
                attp[pr] = S["v"]
                if j + 1 < NT:
                    dma("gpsimd", "rlB", R(qfp(pr)), R(sp_qf[j + 1, pr]),
                        wait=W(("t", gmm)))
                    dma("gpsimd", "rlB", R(nmr(pr)), R(sp_nm[j + 1, pr]),
                        wait=W(("v", ndfree[h % 2])))
        rl_carry[0] = S["rlB"]
        att_v = S["v"]
        p2_tail_t = S["t"]
        p2_tail_v = S["v"]
        for half in range(2):
            tg = load_w_grouped(wo, half, wfree_wo)
            stops = [None] * 4
            for fc in range(FC):
                for oc_ in range(4):
                    w0 = ()
                    if oc_ == 0 and fc % 2 == 0:
                        w0 = W(tg[fc // 2])
                    if oc_ == 0:
                        w0 = w0 + (("v", attp[fc]),)
                    if fc == 0:
                        w0 = w0 + (("v", wo_evac[oc_]),)
                        if j == 0 and half == 0 and oc_ == 0:
                            w0 = w0 + (("s", p2_entry_s),)
                    m_ = mm(wres[:, fc * TT + oc_ * P:fc * TT + oc_ * P + P], qk(fc),
                            psb[4 + oc_][:, :], fc == 0, fc == FC - 1, wait=w0)
                    if fc == FC - 1:
                        stops[oc_] = m_
                wfree_wo[fc] = S["t"]
            for oc_ in range(4):
                oc = half * 4 + oc_
                wo_evac[oc_] = vop(
                    lambda e, o=strm[oc][:, j * TT:(j + 1) * TT], b_=psb[4 + oc_]:
                    e.tensor_tensor(R(o), o, b_[:, :], ALU.add),
                    wait=W(("t", stops[oc_])))

    sv3 = ln_stats()
    td3 = ffn(sv3)

    out_st = [0, 0]
    obank_s = [0, 0, 0, 0]
    first_o = True
    for it in range(16):
        stg = tstage[it % 2]
        osem = "soA" if it % 2 == 0 else "soB"
        for fc in range(FC):
            bi = fc % 4
            bank = psb[bi]
            w0 = W(("s", obank_s[bi]))
            if fc == 0 and it % 4 == 0:
                w0 = w0 + (("v", td3[it // 4]),)
            if first_o:
                w0 = w0 + (("s", S["s"]),)
                first_o = False
            t_ = tp(bank[:, 0:P], strm[fc][:, it * P:(it + 1) * P], P, wait=w0)
            w0 = W(("t", t_))
            if fc == 0:
                w0 = w0 + ((osem, out_st[it % 2]),)
            sop(lambda e, b_=bank, d=stg[:, fc * P:(fc + 1) * P]:
                e.activation(R(d), b_[:, 0:P], AF.Copy), wait=w0)
            obank_s[bi] = S["s"]
        out_st[it % 2] = dma("gpsimd", osem, out[it * P:(it + 1) * P, :], stg,
                             wait=W(("s", S["s"])))
    emit("gpsimd", lambda e: e.nop(), wait=W(("soA", out_st[0]), ("soB", out_st[1])))

    with nc.Block() as block:
        def runner(name):
            def _f(eng):
                for (waits, build, inc, inc_n) in OPS[name]:
                    for (sname, val) in waits:
                        if val > 0:
                            eng.wait_ge(sems[sname], val)
                    ins = build(eng)
                    if inc is not None:
                        if ins is None:
                            eng.sem_inc(sems[inc], inc_n)
                        else:
                            ins.then_inc(sems[inc], inc_n)
            return _f
        block.tensor(runner("tensor"))
        block.vector(runner("vector"))
        block.scalar(runner("scalar"))
        block.sync(runner("sync"))
        block.gpsimd(runner("gpsimd"))

    es.close()
    return nc


def kernel(L0_attn_x, L0_attn_Wq, L0_attn_Wk, L0_attn_Wv, L0_attn_Wo, L0_attn_proj,
           L0_ffn_base_w, L0_ffn_base_b, L0_ffn_lora_A, L0_ffn_lora_B,
           L0_ffn_router_w, L0_ffn_router_b):
    f32 = lambda a: np.ascontiguousarray(np.asarray(a, dtype=np.float32))
    x = f32(L0_attn_x)
    B, Sq, _ = x.shape
    wq, wk, wv, wo = f32(L0_attn_Wq), f32(L0_attn_Wk), f32(L0_attn_Wv), f32(L0_attn_Wo)
    proj = f32(L0_attn_proj)
    bwT = np.ascontiguousarray(f32(L0_ffn_base_w).T)
    bb8 = np.ascontiguousarray(f32(L0_ffn_base_b).reshape(1, DM))
    ra = np.ascontiguousarray(np.transpose(f32(L0_ffn_lora_A).reshape(8, P, 8),
                                           (1, 0, 2)).reshape(P, 64))
    rwp = np.ascontiguousarray(np.transpose(f32(L0_ffn_router_w).T.reshape(8, P, 8),
                                            (1, 0, 2)).reshape(P, 64))
    rb = f32(L0_ffn_router_b).reshape(8, 1)
    lbf = np.ascontiguousarray(f32(L0_ffn_lora_B).reshape(64, DM))

    mskT = np.triu(np.ones((P, P), dtype=np.float32))
    idn = np.eye(P, dtype=np.float32)
    onesw = np.ones((P, TT), dtype=np.float32)
    hfc = np.full((P, P), 0.5, dtype=np.float32)
    repg = np.zeros((40, 64), dtype=np.float32)
    repm = np.zeros((40, 64), dtype=np.float32)
    for c in range(64):
        for r0 in (0, 32):
            repg[r0 + c // 8, c] = 1.0
            repm[r0 + c % 8, c] = 1.0
    cst = np.zeros((P, 8), dtype=np.float32)
    cst[:, C_EPS_LN] = 1e-5
    cst[:, C_EPS_DEN] = 1e-6
    cst[:, C_NLN2] = -LN2
    cst[:, C_PLN2] = LN2
    cst[:, C_M7LN2] = -7.0 * LN2

    nc = build_nc()
    in_maps = []
    for c in range(NC):
        b, r = c // 4, c % 4
        ccf = np.zeros((P, 8), dtype=np.float32)
        for c2 in range(NC):
            if c2 // 4 == b and c2 % 4 < r:
                ccf[:, c2] = 1.0
        in_maps.append({
            "xin": np.ascontiguousarray(x[b, r * T:(r + 1) * T, :]),
            "hf": hfc,
            "wq": wq, "wk": wk, "wv": wv, "wo": wo, "bwT": bwT, "bb8": bb8,
            "ra": ra, "rwp": rwp, "rb": rb, "lb": lbf,
            "prj": np.ascontiguousarray(np.vstack([proj, proj])),
            "mskT": mskT, "idn": idn, "onesw": onesw, "repg": repg, "repm": repm,
            "cst": cst, "ccf": ccf,
        })
    res = run_bass_kernel_spmd(nc, in_maps, core_ids=list(range(NC)))
    y = np.empty((B, Sq, DM), dtype=np.float32)
    for c in range(NC):
        b, r = c // 4, c % 4
        y[b, r * T:(r + 1) * T, :] = res.results[c]["out"]
    return y


if __name__ == "__main__":
    import sys
    if "--build" in sys.argv:
        build_nc()
        print("build ok")


# version 45
# speedup vs baseline: 1.0087x; 1.0087x over previous
import numpy as np
from contextlib import ExitStack

import concourse.bass as bass
import concourse.mybir as mybir
from concourse.bass_utils import run_bass_kernel_spmd

F32 = mybir.dt.float32
F32R = mybir.dt.float32r
AF = mybir.ActivationFunctionType
ALU = mybir.AluOpType

P = 128
DM = 1024
FC = 8
T = 2048
TT = 512
NT = 4
H = 16
D = 64
NC = 8
LN2 = 0.6931471805599453
INV_LN2 = 1.4426950408889634
C_EPS_LN, C_EPS_DEN, C_NLN2, C_PLN2, C_M7LN2 = 0, 1, 2, 3, 4
AOFF = [0, 512, 896, 1152]
AW = [512, 384, 256, 128]


def build_nc():
    nc = bass.Bass("TRN2", target_bir_lowering=False)

    xin = nc.dram_tensor("xin", [T, DM], F32, kind="ExternalInput").ap()
    wq = nc.dram_tensor("wq", [DM, DM], F32, kind="ExternalInput").ap()
    wk = nc.dram_tensor("wk", [DM, DM], F32, kind="ExternalInput").ap()
    wv = nc.dram_tensor("wv", [DM, DM], F32, kind="ExternalInput").ap()
    wo = nc.dram_tensor("wo", [DM, DM], F32, kind="ExternalInput").ap()
    bwT = nc.dram_tensor("bwT", [DM, DM], F32, kind="ExternalInput").ap()
    bb8 = nc.dram_tensor("bb8", [1, DM], F32, kind="ExternalInput").ap()
    ra = nc.dram_tensor("ra", [P, 64], F32, kind="ExternalInput").ap()
    rwp = nc.dram_tensor("rwp", [P, 64], F32, kind="ExternalInput").ap()
    rb = nc.dram_tensor("rb", [8, 1], F32, kind="ExternalInput").ap()
    lb = nc.dram_tensor("lb", [64, DM], F32, kind="ExternalInput").ap()
    prj = nc.dram_tensor("prj", [P, D], F32, kind="ExternalInput").ap()
    mskT = nc.dram_tensor("mskT", [P, P], F32, kind="ExternalInput").ap()
    idn = nc.dram_tensor("idn", [P, P], F32, kind="ExternalInput").ap()
    onesw = nc.dram_tensor("onesw", [P, TT], F32, kind="ExternalInput").ap()
    repg = nc.dram_tensor("repg", [40, 64], F32, kind="ExternalInput").ap()
    repm = nc.dram_tensor("repm", [40, 64], F32, kind="ExternalInput").ap()
    hf = nc.dram_tensor("hf", [P, P], F32, kind="ExternalInput").ap()
    cst = nc.dram_tensor("cst", [P, 8], F32, kind="ExternalInput").ap()
    ccf = nc.dram_tensor("ccf", [P, 8], F32, kind="ExternalInput").ap()
    out = nc.dram_tensor("out", [T, DM], F32, kind="ExternalOutput").ap()

    sp_qf = nc.dram_tensor("sp_qf", [NT, FC, P, TT], F32).ap()
    sp_nm = nc.dram_tensor("sp_nm", [NT, FC, P, DM], F32).ap()
    cc_in = nc.dram_tensor("cc_in", [P, 528], F32).ap()
    cc_out = nc.dram_tensor("cc_out", [NC, P, 528], F32, addr_space="Shared").ap()

    es = ExitStack()
    es.enter_context(nc.allow_low_precision(reason="float32r staging for PE operands"))
    sb = lambda name, shp: es.enter_context(nc.sbuf_tensor(name, shp, F32))
    psx = lambda name: es.enter_context(nc.psum_tensor(name, [P, TT], F32))

    strm = [sb(f"strm{i}", [P, T]) for i in range(FC)]
    wres = sb("wres", [P, 4096])
    big0 = sb("big0", [P, 8448])
    big1 = sb("big1", [P, 4096])
    qft = sb("qft", [P, 4096])
    big3 = sb("big3", [P, 5120])
    sts = sb("sts", [P, 2112])
    stt_ = sb("stt", [P, 2048])
    gsc = sb("gsc", [P, 1152])
    rcs = sb("rcs", [40, 2])
    wblk = sb("wblk", [P, DM])
    wblk2 = sb("wblk2", [P, DM])
    lbt = sb("lbt", [64, DM])
    prjt = sb("prjt", [P, D])
    mskt = sb("mskt", [P, P])
    idnt = sb("idnt", [P, P])
    onet = sb("onet", [P, TT])
    repgt = sb("repgt", [40, 64])
    repmt = sb("repmt", [40, 64])
    cstt = sb("cstt", [P, 8])
    ccft = sb("ccft", [P, 8])
    bb8t = sb("bb8t", [1, DM])
    rbt = sb("rbt", [8, 1])
    hfr = sb("hfr", [P, P])
    rwp_t = sb("rwp_t", [P, 64])
    ra_t = sb("ra_t", [P, 64])

    uA = lambda fc: big0[:, fc * TT:(fc + 1) * TT]
    uB = lambda fc: big0[:, 4096 + fc * TT:4096 + (fc + 1) * TT]
    uC = lambda fc: big1[:, fc * TT:(fc + 1) * TT]
    uD = lambda fc: qft[:, fc * TT:(fc + 1) * TT]
    ktk = lambda s: big0[:, s * 1024:(s + 1) * 1024]
    qk = lambda pr: big0[:, 4096 + pr * TT:4096 + (pr + 1) * TT]
    vtok = lambda s: big0[:, 4096 + s * 1056:4096 + (s + 1) * 1056]
    vsl = lambda s, h: big0[:, 4096 + s * 1056 + h * 66:4096 + s * 1056 + h * 66 + 65]
    vsl66 = lambda s, h: big0[:, 4096 + s * 1056 + h * 66:4096 + s * 1056 + h * 66 + 66]
    tstage = [big0[:, 4096 + i * DM:4096 + (i + 1) * DM] for i in range(2)]
    kfp = lambda pr: big1[:, pr * TT:(pr + 1) * TT]
    qfp = lambda pr: qft[:, pr * TT:(pr + 1) * TT]
    hsl = lambda t_, h: t_[(h % 2) * D:(h % 2) * D + D, (h // 2) * TT:(h // 2 + 1) * TT]
    atm = lambda c0, w: big3[:, c0:c0 + w]
    atm2 = lambda hh, c0, w: big3[:, (0 if hh % 2 == 0 else 3840) + c0:
                                  (0 if hh % 2 == 0 else 3840) + c0 + w]
    scr1 = big3[:, 0:512]
    scr2 = big3[:, 512:1024]
    nst = lambda k: big3[:, 1280 + k * DM:1280 + (k + 1) * DM]
    Gt = big3[0:64, 3328:3840]
    band = lambda i: big3[0:8, 512 + i * 512:1024 + i * 512]
    brow = lambda i: big3[0:1, 512 + i * 512:1024 + i * 512]
    recrow = lambda par: big3[64 * par:64 * par + 1, 0:512]
    sA = sts[:, 0:528]
    sPr = sts[:, 528:1056]
    gB = sts[:, 1056:1584]
    gR = sts[:, 1584:2112]
    ssl = lambda t_, h: t_[(h % 2) * D:(h % 2) * D + D, (h // 2) * 66:(h // 2) * 66 + 65]

    psb = [psx(f"pb{i}") for i in range(8)]

    sems = {}
    S = {}
    for s_ in ("t", "v", "s", "g", "cc", "sm", "iA", "iB", "wb", "wbB", "wr", "wrB",
               "wrC", "wrD", "rl", "rlB", "sq", "snA", "snB", "sc", "soA", "soB"):
        sems[s_] = es.enter_context(nc.semaphore(f"sem_{s_}"))
        S[s_] = 0
    OPS = {k: [] for k in ("tensor", "vector", "scalar", "sync", "gpsimd")}

    def emit(eng, build, wait=(), inc=None, inc_n=1):
        for (sname, val) in wait:
            assert val <= S[sname], f"wait on future {sname}:{val}>{S[sname]}"
        if inc is not None:
            S[inc] += inc_n
        OPS[eng].append((tuple(wait), build, inc, inc_n))
        return S[inc] if inc is not None else None

    W = lambda *pairs: tuple(pairs)
    R = lambda ap: ap.bitcast(F32R)

    def dma(queue, sem, dst, src, wait=()):
        return emit(queue, lambda e, d=dst, s=src: e.dma_start(out=d, in_=s),
                    wait=wait, inc=sem, inc_n=16)

    import traceback as _tb
    MMLOG = getattr(build_nc, "MMLOG", None)
    def mm(lhsT, rhs, outp, start, stop, wait=(), dt=F32R):
        if MMLOG is not None:
            fr = _tb.extract_stack(limit=3)[0]
            MMLOG.append((outp.tensor.name, outp.offset, tuple(map(tuple, outp.ap)),
                          start, stop, fr.lineno))
        if dt is F32R:
            lhsT, rhs = lhsT.bitcast(F32R), rhs.bitcast(F32R)
        return emit("tensor",
                    lambda e, l=lhsT, r=rhs, o=outp, a=start, b=stop:
                    e.matmul(o, l, r, start=a, stop=b), wait=wait, inc="t")

    def tp(outp, inp, n, wait=()):
        if MMLOG is not None:
            MMLOG.append((outp.tensor.name, outp.offset, tuple(map(tuple, outp.ap)),
                          "TP", "TP", 0))
        return emit("tensor", lambda e, o=outp, i=inp, n_=n: e.transpose(o, i, idnt[0:n_, 0:n_]),
                    wait=wait, inc="t")

    def vop(build, wait=()):
        return emit("vector", build, wait=wait, inc="v")

    def sop(build, wait=()):
        return emit("scalar", build, wait=wait, inc="s")

    def gop(build, wait=()):
        return emit("gpsimd", build, wait=wait, inc="g")

    def bounce(eng, sem, wait):
        return emit(eng, lambda e: None, wait=wait, inc=sem)

    for dst, src_, cast in ((lbt, lb, 1), (prjt, prj, 1), (mskt, mskT, 0), (idnt, idn, 0),
                            (onet, onesw, 1), (repgt, repg, 1), (repmt, repm, 1),
                            (cstt, cst, 0), (ccft, ccf, 0), (bb8t, bb8, 1), (rbt, rb, 0),
                            (rwp_t, rwp, 1), (ra_t, ra, 1), (hfr, hf, 1)):
        if cast:
            dma("sync", "sm", R(dst[:]), R(src_[:]))
        else:
            dma("sync", "sm", dst[:], src_[:])
    ld_smalls = S["sm"]
    vop(lambda e: e.tensor_scalar(R(bb8t[:]), bb8t[:], 0.5, 0.0, ALU.mult, ALU.bypass),
        wait=W(("sm", ld_smalls)))
    for k_ in range(2):
        vop(lambda e, k2=k_: e.tensor_scalar(R(nst(k2)[:, TT:2 * TT]), onet[:, 0:TT],
                                             0.0, 0.0, ALU.mult, ALU.bypass))



    def st_row(t_, which):
        base = 32 * t_ if t_ < 3 else 0
        if which in (1, 2):
            c0 = (0 if t_ < 3 else 1024) + (0 if which == 2 else 512)
            return stt_[base:base + 1, c0:c0 + 512]
        c0 = 0 if which == 0 else 512
        return gsc[base:base + 1, c0:c0 + 512]

    def ln_stats_tile(t_, pb=(2, 3), wv=None, ws=None, psfree=0, alt_tr=False,
                      xs_fn=None):
        tc = slice(t_ * TT, (t_ + 1) * TT)
        base = 32 * t_ if t_ < 3 else 0
        for fc in range(FC):
            xs = big1[:, (fc % 2) * TT:(fc % 2) * TT + TT] if xs_fn is None \
                else xs_fn(fc % 2)
            sqw = W(("t", S["t"]), ("v", S["v"] if wv is None else wv))
            if ws is not None and fc == 0:
                sqw = sqw + (("s", ws),)
            s_ = sop(lambda e, o=xs, i=strm[fc][:, tc]: e.activation(R(o), i, AF.Square),
                     wait=sqw)
            w0 = ()
            if fc == 0:
                w0 = W(("v", max(psfree, S["v"] if wv is None else wv)))
                if ws is not None:
                    w0 = w0 + (("s", ws),)
            mm(onet[:, 0:1], strm[fc][:, tc], psb[pb[0]][0:1, :], fc == 0, fc == FC - 1,
               wait=w0)
            mm(onet[:, 0:1], xs, psb[pb[1]][0:1, :], fc == 0, fc == FC - 1,
               wait=W(("s", s_)))
        tmm = S["t"]
        if alt_tr:
            rm = big1[base:base + 1, 2048:2560]
            re = big1[base:base + 1, 2560:3072]
            rsm, rs = st_row(t_, 1), st_row(t_, 2)
        else:
            rm, rsm, rs, re = (st_row(t_, i) for i in range(4))
        vop(lambda e, o=rm: e.tensor_scalar(R(o), psb[pb[0]][0:1, :], 1.0 / DM, 0.0,
                                            ALU.mult, ALU.bypass), wait=W(("t", tmm)))
        vop(lambda e, o=re: e.tensor_scalar(R(o), psb[pb[1]][0:1, :], 1.0 / DM, 0.0,
                                            ALU.mult, ALU.bypass))
        b1 = bounce("scalar", "s", W(("v", S["v"])))
        vop(lambda e, o=rs, a=rm: e.tensor_tensor(R(o), a, a, ALU.mult),
            wait=W(("s", b1)))
        b2x = bounce("scalar", "s", W(("v", S["v"])))
        vop(lambda e, o=re, b_=rs: e.tensor_tensor(R(o), o, b_, ALU.subtract),
            wait=W(("s", b2x)))
        vv = S["v"]
        ss = sop(lambda e, o=rs, i=re, b2=base: e.activation(
            R(o), i, AF.Sqrt, bias=cstt[b2:b2 + 1, C_EPS_LN:C_EPS_LN + 1]), wait=W(("v", vv)))
        vop(lambda e, o=rs: e.reciprocal(R(o), o), wait=W(("s", ss)))
        b3 = bounce("scalar", "s", W(("v", S["v"])))
        vop(lambda e, o=rsm, a=rm, b_=rs: e.tensor_tensor(R(o), a, b_, ALU.mult),
            wait=W(("s", b3)))
        return S["v"]

    def ln_stats(tile_waits=None, xs_fn=None):
        out = []
        psfree = 0
        for t_ in range(NT):
            twv = tile_waits[t_] if tile_waits is not None else None
            out.append(ln_stats_tile(t_, wv=twv, psfree=psfree, xs_fn=xs_fn))
            psfree = out[-1]
        return out

    def make_u(t_, stats_v, dst, vguard=None, extra=(), row=None):
        row = onet if row is None else row
        tc = slice(t_ * TT, (t_ + 1) * TT)
        ob = 32 * t_ if t_ < 3 else 0
        if vguard is None:
            vguard = S["v"]
        mm(row[ob:ob + 1, 0:P], st_row(t_, 2), psb[4][:, :], True, True,
           wait=W(("v", max(stats_v, vguard)), ("t", S["t"])) + tuple(extra))
        bt = mm(row[ob:ob + 1, 0:P], st_row(t_, 1), psb[5][:, :], True, True)
        for fc in range(FC):
            vop(lambda e, o=dst(fc), i=strm[fc][:, tc]:
                e.tensor_tensor(R(o), i, psb[4][:, :], ALU.mult),
                wait=W(("t", bt)) if fc == 0 else ())
        bu_ = bounce("scalar", "s", W(("v", S["v"])))
        for fc in range(FC):
            vop(lambda e, o=dst(fc): e.tensor_tensor(R(o), o, psb[5][:, :], ALU.subtract),
                wait=W(("s", bu_)) if fc == 0 else ())
        return S["v"]

    GS = lambda k: gsc[:, k * P:(k + 1) * P]
    S_G, S_U, S_Q, S_R, S_ACC, S_B, S_F, S_P, S_B0 = range(9)
    bnd = lambda i, r: big3[r:r + 8, 512 + i * 512:1024 + i * 512]

    def chain(steps, init_wait=None):
        prev = init_wait
        for forced, build in steps:
            eng = forced if forced else "vector"
            if prev is not None and prev[0] == {"vector": "v", "gpsimd": "g",
                                                "scalar": "s"}[eng]:
                ob = "scalar" if eng != "scalar" else "vector"
                bsem = {"vector": "v", "gpsimd": "g", "scalar": "s"}[ob]
                prev = (bsem, bounce(ob, bsem, W(prev)))
            sem = {"vector": "v", "gpsimd": "g", "scalar": "s"}[eng]
            val = emit(eng, build, wait=W(prev) if prev else (), inc=sem)
            prev = (sem, val)
        return prev

    def colsums():
        for fc in range(FC):
            mm(rwp_t[:, fc * 8:fc * 8 + 8], onet[:, 0:1], psb[2][0:8, 0:1],
               fc == 0, fc == FC - 1, dt=F32,
               wait=W(("sm", ld_smalls)) if fc == 0 else ())
            mm(ra_t[:, fc * 8:fc * 8 + 8], onet[:, 0:1], psb[3][0:8, 0:1],
               fc == 0, fc == FC - 1, dt=F32)
        tmm = S["t"]
        vop(lambda e: e.tensor_copy(rcs[0:8, 0:1], psb[2][0:8, 0:1]), wait=W(("t", tmm)))
        vop(lambda e: e.tensor_scalar(rcs[0:8, 1:2], psb[3][0:8, 0:1], -1.0, 0.0,
                                      ALU.mult, ALU.bypass))
        vop(lambda e: e.tensor_copy(rcs[32:40, 0:1], psb[2][0:8, 0:1]))
        vop(lambda e: e.tensor_scalar(rcs[32:40, 1:2], psb[3][0:8, 0:1], -1.0, 0.0,
                                      ALU.mult, ALU.bypass))

    def gating_setup_tile(t_, wv, extra=(), banks=(2, 3, 6, 7)):
        if True:
            tc = slice(t_ * TT, (t_ + 1) * TT)
            lb_, mb_, sb_, smb_ = (psb[b_] for b_ in banks)
            for fc in range(FC):
                w0 = (W(("v", wv)) + tuple(extra)) if fc == 0 else ()
                mm(rwp_t[:, fc * 8:fc * 8 + 8], strm[fc][:, tc], lb_[0:8, :],
                   fc == 0, fc == FC - 1, wait=w0)
            for fc in range(FC):
                mm(ra_t[:, fc * 8:fc * 8 + 8], strm[fc][:, tc], mb_[0:8, :],
                   fc == 0, fc == FC - 1)
            ob2 = 32 * t_ if t_ < 3 else 0
            mm(onet[ob2:ob2 + 1, 0:8], st_row(t_, 2), sb_[0:8, :], True, True)
            bt = mm(onet[ob2:ob2 + 1, 0:8], st_row(t_, 1), smb_[0:8, :], True, True)
            vop(lambda e: e.tensor_copy(R(bnd(0, 0)), lb_[0:8, :]), wait=W(("t", bt)))
            b0x = bounce("scalar", "s", W(("v", S["v"])))
            vop(lambda e: e.tensor_tensor(R(bnd(0, 0)), bnd(0, 0), sb_[0:8, :], ALU.mult),
                wait=W(("s", b0x)))
            b1 = bounce("scalar", "s", W(("v", S["v"])))
            vop(lambda e: e.scalar_tensor_tensor(R(bnd(0, 0)), smb_[0:8, :],
                rcs[0:8, 0:1], bnd(0, 0), ALU.mult, ALU.subtract), wait=W(("s", b1)))
            vv = S["v"]
            ss = sop(lambda e: e.activation(R(bnd(0, 0)), bnd(0, 0), AF.Exp,
                                            scale=-1.0, bias=rbt[0:8, 0:1]),
                     wait=W(("v", vv)))
            mv = vop(lambda e, t2=t_: e.tensor_copy(R(bnd(1 + t2 // 2, (t2 % 2) * 32)),
                                                    mb_[0:8, :]), wait=W(("s", ss)))
            b1x = bounce("scalar", "s", W(("v", S["v"])))
            vop(lambda e, t2=t_: e.tensor_tensor(
                R(bnd(1 + t2 // 2, (t2 % 2) * 32)), bnd(1 + t2 // 2, (t2 % 2) * 32),
                sb_[0:8, :], ALU.mult), wait=W(("s", b1x)))
            b2 = bounce("scalar", "s", W(("v", S["v"])))
            vop(lambda e, t2=t_: e.scalar_tensor_tensor(
                R(bnd(1 + t2 // 2, (t2 % 2) * 32)), smb_[0:8, :],
                rcs[(t2 % 2) * 32:(t2 % 2) * 32 + 8, 1:2],
                bnd(1 + t2 // 2, (t2 % 2) * 32), ALU.mult, ALU.add), wait=W(("s", b2)))
            smm = mm(onet[0:8, 0:1], bnd(0, 0), lb_[0:1, :], True, True,
                     wait=W(("s", ss), ("v", S["v"])))
            vop(lambda e: e.reciprocal(R(bnd(3, 0)[0:1, :]), lb_[0:1, :]),
                wait=W(("t", smm)))
            vv = S["v"]
            bmm = mm(onet[0:1, 0:8], bnd(3, 0)[0:1, :], lb_[0:8, :], True, True,
                     wait=W(("v", vv)))
            vop(lambda e: e.tensor_tensor(R(bnd(0, 0)), bnd(0, 0), lb_[0:8, :],
                                          ALU.mult), wait=W(("t", bmm)))
            for k in range(4):
                bank = psb[4 + k % 2]
                tp_ = tp(bank[:, 0:8], bnd(0, 0)[0:8, k * P:(k + 1) * P], 8,
                         wait=W(("v", S["v"])))
                vop(lambda e, b_=bank, c=t_ * 32 + k * 8: e.tensor_copy(
                    GS(S_G)[:, c:c + 8], b_[:, 0:8]), wait=W(("t", tp_)))

    def gating_chain():
        ts_ = lambda o, i, s1, s2, o1, o2: (lambda e: e.tensor_scalar(
            GS(o), GS(i), s1, s2, o1, o2))
        tt_ = lambda o, a, b: (lambda e: e.tensor_tensor(GS(o), GS(a), GS(b), ALU.mult))
        steps = []
        steps.append(("scalar", lambda e: e.activation(GS(S_U), GS(S_G), AF.Ln)))
        steps.append((None, ts_(S_U, S_U, -INV_LN2, 0.0, ALU.mult, ALU.bypass)))
        steps.append((None, ts_(S_Q, S_U, 0.0, 2.0, ALU.mult, ALU.add)))
        for v in (32, 16, 8, 4, 2, 1):
            steps.append(("vector", ts_(S_B, S_U, float(v), 0.0, ALU.is_ge, ALU.bypass)))
            steps.append((None, lambda e, v2=float(v): e.scalar_tensor_tensor(
                GS(S_U), GS(S_B), -v2, GS(S_U), ALU.mult, ALU.add)))
            steps.append((None, ts_(S_F, S_B, float(2 ** v - 1), 1.0, ALU.mult, ALU.add)))
            steps.append((None, tt_(S_Q, S_Q, S_F)))
        steps.append((None, lambda e: e.scalar_tensor_tensor(
            GS(S_R), GS(S_G), 128.0, GS(S_Q), ALU.mult, ALU.mult)))
        steps.append(("vector", ts_(S_B, S_R, 128.0, 0.0, ALU.is_lt, ALU.bypass)))
        steps.append((None, ts_(S_P, S_B, -0.5, 1.0, ALU.mult, ALU.add)))
        steps.append((None, ts_(S_F, S_B, 1.0, 1.0, ALU.mult, ALU.add)))
        steps.append((None, tt_(S_R, S_R, S_F)))
        steps.append(("vector", ts_(S_B, S_R, 256.0, 0.0, ALU.is_ge, ALU.bypass)))
        steps.append((None, ts_(S_F, S_B, -0.5, 1.0, ALU.mult, ALU.add)))
        steps.append((None, tt_(S_R, S_R, S_F)))
        steps.append((None, ts_(S_F, S_B, 1.0, 1.0, ALU.mult, ALU.add)))
        steps.append((None, tt_(S_P, S_P, S_F)))
        steps.append((None, ts_(S_R, S_R, -128.0, 0.0, ALU.add, ALU.bypass)))
        steps.append((None, ts_(S_ACC, S_R, 0.0, 0.0, ALU.mult, ALU.bypass)))
        for v in (64, 32, 16, 8, 4, 2, 1):
            steps.append(("vector", ts_(S_B, S_R, float(v), 0.0, ALU.is_ge, ALU.bypass)))
            steps.append((None, lambda e, v2=float(v): e.scalar_tensor_tensor(
                GS(S_R), GS(S_B), -v2, GS(S_R), ALU.mult, ALU.add)))
            steps.append((None, lambda e, v2=float(v): e.scalar_tensor_tensor(
                GS(S_ACC), GS(S_B), v2, GS(S_ACC), ALU.mult, ALU.add)))
        steps.append((None, ts_(S_B0, S_B, 1.0, 0.0, ALU.mult, ALU.bypass)))
        steps.append(("vector", ts_(S_F, S_R, 0.5 + 2.0 ** -17, 0.0, ALU.is_ge, ALU.bypass)))
        steps.append(("vector", ts_(S_U, S_R, 0.5 - 2.0 ** -17, 0.0, ALU.is_ge, ALU.bypass)))
        steps.append((None, lambda e: e.scalar_tensor_tensor(
            GS(S_U), GS(S_F), -1.0, GS(S_U), ALU.mult, ALU.add)))
        steps.append((None, tt_(S_U, S_U, S_B0)))
        steps.append((None, lambda e: e.scalar_tensor_tensor(
            GS(S_F), GS(S_F), 1.0, GS(S_U), ALU.mult, ALU.add)))
        steps.append((None, ts_(S_ACC, S_ACC, 128.0, 0.0, ALU.add, ALU.bypass)))
        steps.append((None, lambda e: e.scalar_tensor_tensor(
            GS(S_ACC), GS(S_ACC), 1.0, GS(S_F), ALU.mult, ALU.add)))
        steps.append(("vector", lambda e: e.reciprocal(GS(S_Q), GS(S_Q))))
        steps.append((None, lambda e: e.scalar_tensor_tensor(
            GS(S_ACC), GS(S_ACC), 0.0078125, GS(S_Q), ALU.mult, ALU.mult)))
        steps.append((None, tt_(S_ACC, S_ACC, S_P)))
        prev = chain(steps, init_wait=("v", S["v"]))
        return prev

    def gating_finish(prev):
        for t_ in range(NT):
            for k in range(4):
                bank = psb[4 + k % 2]
                w0 = W(prev, ("v", S["v"])) if (t_ == 0 and k == 0) else W(("v", S["v"]))
                tp_ = tp(bank[0:8, 0:P],
                         GS(S_ACC)[:, t_ * 32 + k * 8:t_ * 32 + k * 8 + 8], P, wait=w0)
                vop(lambda e, b_=bank, t2=t_, k2=k: e.tensor_copy(
                    R(bnd(3 + t2 // 2, (t2 % 2) * 32)[0:8, k2 * P:(k2 + 1) * P]),
                    b_[0:8, 0:P]), wait=W(("t", tp_)))
        return S["v"]

    def G_for(t_):
        vv = S["v"]
        mm(repgt[(t_ % 2) * 32:(t_ % 2) * 32 + 8, :], bnd(3 + t_ // 2, (t_ % 2) * 32),
           psb[6][0:64, :], True, True, wait=W(("v", vv)))
        m2 = mm(repmt[(t_ % 2) * 32:(t_ % 2) * 32 + 8, :], bnd(1 + t_ // 2, (t_ % 2) * 32),
                psb[7][0:64, :], True, True)
        vop(lambda e: e.tensor_copy(R(Gt), psb[7][0:64, :]), wait=W(("t", m2)))
        bG = bounce("scalar", "s", W(("v", S["v"])))
        vop(lambda e: e.tensor_tensor(R(Gt), Gt, psb[6][0:64, :], ALU.mult),
            wait=W(("s", bG)))
        return S["v"]

    def load_whalf(src, half, wait=()):
        last = None
        for fc in range(FC):
            last = dma("sync", "wr", R(wres[:, fc * TT:(fc + 1) * TT]),
                       R(src[fc * P:(fc + 1) * P, half * TT:(half + 1) * TT]),
                       wait=wait if fc == 0 else ())
        return ("wr", last)

    WRS = ("wr", "wrB", "wrC", "wrD")

    def load_w_grouped(src, half, wfree):
        tg = []
        for g in range(4):
            for k in range(2):
                fc = 2 * g + k
                dma("sync", WRS[g], R(wres[:, fc * TT:(fc + 1) * TT]),
                    R(src[fc * P:(fc + 1) * P, half * TT:(half + 1) * TT]),
                    wait=W(("t", wfree[fc])))
            tg.append((WRS[g], S[WRS[g]]))
        return tg

    def ffn(stats_tiles, s_guard=None, setup_ext=False, setup_banks=(2, 3, 6, 7),
            setup_guard=None):
        wfree = [S["t"]] * FC
        s_entry = S["s"] if s_guard is None else s_guard
        u4 = (uA, uB, uC, uD)
        uvs = [make_u(t_, stats_tiles[t_], u4[t_], row=hfr) for t_ in range(NT)]
        ev01 = S["v"]
        if not setup_ext:
            psg = S["v"] if setup_guard is None else max(setup_guard, uvs[NT - 1])
            for t_ in range(NT):
                gating_setup_tile(t_, max(stats_tiles[t_], psg),
                                  banks=setup_banks)
                psg = S["v"]
        evac_v = [ev01, ev01, S["v"], S["v"]]
        chain_prev = gating_chain()

        def lora_pass(ocs):
            bfree = [None, None]
            tile_done = [0] * NT
            for t_ in range(NT):
                gv = G_for(t_)
                tc = slice(t_ * TT, (t_ + 1) * TT)
                for i, oc in enumerate(ocs):
                    bank = psb[4 + i % 2]
                    w0 = W(("v", gv)) if i == 0 else ()
                    if bfree[i % 2] is not None and i >= 2:
                        w0 = w0 + (("v", bfree[i % 2]),)
                    lmm = mm(lbt[:, oc * P:(oc + 1) * P], Gt, bank[:, :], True, True,
                             wait=w0)
                    bfree[i % 2] = vop(lambda e, o=strm[oc][:, tc], b_=bank:
                        e.tensor_tensor(R(o), o, b_[:, :], ALU.add),
                        wait=W(("t", lmm)))
                tile_done[t_] = S["v"]
            return tile_done

        for half in range(2):
            tg = load_w_grouped(bwT, half, wfree)
            for t_ in range(NT):
                ufn = u4[t_]
                tc = slice(t_ * TT, (t_ + 1) * TT)
                for fc in range(FC):
                    for oc_ in range(4):
                        w0 = ()
                        if t_ == 0 and oc_ == 0 and fc % 2 == 0:
                            w0 = W(tg[fc // 2])
                        if fc == 0:
                            w0 = w0 + (("v", max(uvs[t_], evac_v[oc_])),)
                            if half == 0 and t_ == 0:
                                w0 = w0 + (("s", s_entry),)
                        mm(wres[:, fc * TT + oc_ * P:fc * TT + oc_ * P + P],
                           ufn(fc), psb[oc_][:, :], fc == 0, False, wait=w0)
                    if t_ == NT - 1:
                        wfree[fc] = S["t"]
                for oc_ in range(4):
                    oc = half * 4 + oc_
                    lmm = mm(bb8t[0:1, oc * P:(oc + 1) * P], onet[0:1, 0:TT],
                             psb[oc_][:, :], False, True)
                    evac_v[oc_] = vop(
                        lambda e, o=strm[oc][:, tc], b_=psb[oc_]:
                        e.tensor_tensor(R(o), o, b_[:, :], ALU.add),
                        wait=W(("t", lmm)))
        gating_finish(chain_prev)
        return lora_pass(tuple(range(2 * NT)))

    colsums()
    stage_s = [0, 0]
    ibank_s = [0, 0, 0, 0]
    sv1_tiles = [0] * NT
    in_tile_s = [0] * NT
    in_psg = [0]
    for it in range(16):
        stg = tstage[it % 2]
        isem = "iA" if it % 2 == 0 else "iB"
        ld = dma("sync", isem, R(stg), R(xin[it * P:(it + 1) * P, :]),
                 wait=W(("s", stage_s[it % 2])))
        for fc in range(FC):
            bi = fc % 4
            bank = psb[bi]
            w0 = W((isem, ld), ("s", ibank_s[bi]))
            if it == 0 and fc == 0:
                w0 = w0 + (("sm", ld_smalls), ("v", S["v"]))
            t_ = tp(bank[:, 0:P], stg[:, fc * P:(fc + 1) * P], P, wait=w0)
            sop(lambda e, b_=bank, d=strm[fc][:, it * P:(it + 1) * P]:
                e.activation(R(d), b_[:, 0:P], AF.Copy), wait=W(("t", t_)))
            ibank_s[bi] = S["s"]
        stage_s[it % 2] = S["s"]
        if it % 4 == 3:
            t2 = it // 4
            tin_s = S["s"]
            sv1_tiles[t2] = ln_stats_tile(t2, pb=(4, 5), ws=tin_s,
                                          psfree=in_psg[0], alt_tr=True)
            gating_setup_tile(t2, sv1_tiles[t2], extra=(("s", tin_s),),
                              banks=(4, 5, 6, 7))
            in_psg[0] = S["v"]
    input_v = S["v"]
    input_s = S["s"]
    td1 = ffn(sv1_tiles, s_guard=input_s, setup_ext=True)

    sv2 = ln_stats(td1)
    qf_spill_st = 0
    nm_spill_st = [0, 0]
    prev_tail_v = None
    prev_tail_s = None
    for j in range(NT):
        uv = make_u(j, sv2[j], uA, vguard=prev_tail_v, extra=(
            (("s", prev_tail_s),) if prev_tail_s else ()))

        def proj_qk(wsrc, w0extra, dst=None):
            if dst is None:
                dst = qk
            used_t = [S["t"], S["t"]]
            for fc in range(FC):
                wb = wblk if fc % 2 == 0 else wblk2
                ws = "wb" if fc % 2 == 0 else "wbB"
                qn = "sync" if fc % 2 == 0 else "gpsimd"
                ld = dma(qn, ws, R(wb[:]), R(wsrc[fc * P:(fc + 1) * P, :]),
                         wait=W(("t", used_t[fc % 2])))
                for oc in range(FC):
                    w0 = W((ws, ld)) if oc == 0 else ()
                    if fc == 0 and oc == 0:
                        w0 = w0 + tuple(w0extra)
                    mm(wb[:, oc * P:(oc + 1) * P], uA(fc), psb[oc][:, :],
                       fc == 0, fc == FC - 1, wait=w0)
                used_t[fc % 2] = S["t"]
            tmm = S["t"]
            fv = fs = True
            for oc in range(FC):
                if oc % 2 == 0:
                    vop(lambda e, o=dst(oc), b_=psb[oc]: e.tensor_copy(R(o), b_[:, :]),
                        wait=W(("t", tmm)) if fv else ())
                    fv = False
                else:
                    sop(lambda e, o=dst(oc), b_=psb[oc]:
                        e.activation(R(o), b_[:, :], AF.Copy),
                        wait=W(("t", tmm)) if fs else ())
                    fs = False
            return S["v"], S["s"]

        def fm(dst_pair, src_v, extra_wait=(), src=None, filler=None):
            sfn = qk if src is None else src
            bank_done = [src_v, src_v]
            for pr in range(FC):
                if filler is not None:
                    filler(pr)
                bs = 4 if pr % 2 == 0 else 6
                sc_ = scr1 if pr % 2 == 0 else scr2
                w0 = W(("v", max(src_v, bank_done[pr % 2])))
                if pr == 0:
                    w0 = w0 + tuple(extra_wait)
                mm(prjt[0:D, :], sfn(pr)[0:D, :], psb[bs][0:D, :], True, True, wait=w0)
                m2 = mm(prjt[D:P, :], sfn(pr)[D:P, :], psb[bs + 1][0:D, :], True, True)
                vop(lambda e, b2=bs, s2=sc_: e.tensor_scalar(
                    R(s2[0:D, :]), psb[b2][0:D, :], 0.0, 0.0, ALU.min, ALU.bypass),
                    wait=W(("t", m2), ("s", S["s"])))
                vop(lambda e, b2=bs, s2=sc_: e.tensor_scalar(
                    R(s2[D:P, :]), psb[b2 + 1][0:D, :], 0.0, 0.0, ALU.min, ALU.bypass))
                vv = S["v"]
                ssx = sop(lambda e, s2=sc_: e.activation(R(s2), s2, AF.Exp),
                          wait=W(("v", vv)))
                vop(lambda e, b2=bs, s2=sc_, o=dst_pair(pr): e.scalar_tensor_tensor(
                    R(o[0:D, :]), psb[b2][0:D, :], 0.0, s2[0:D, :], ALU.max, ALU.add),
                    wait=W(("s", ssx)))
                vop(lambda e, b2=bs, s2=sc_, o=dst_pair(pr): e.scalar_tensor_tensor(
                    R(o[D:P, :]), psb[b2 + 1][0:D, :], 0.0, s2[D:P, :], ALU.max, ALU.add))
                bank_done[pr % 2] = S["v"]
            return S["v"]

        qv, qs_ = proj_qk(wq, (("v", uv),))
        kv_, ks_ = proj_qk(wk, (("v", qv), ("s", qs_)), dst=kfp)

        def vproj_wave_evacs(w_, stop_all):
            for ti in range(2):
                ts = 2 * w_ + ti
                for pc in range(2):
                    def ev(e, t2=ts, p2=pc, b2=ti * 2 + pc):
                        src = psb[b2][:, :].rearrange("p (h d) -> p h d", h=8)
                        dst = vtok(t2)[:, p2 * 528:(p2 + 1) * 528].rearrange(
                            "p (h s) -> p h s", h=8)[:, :, 0:64]
                        return e.tensor_copy(R(dst), src)
                    vop(ev, wait=W(("t", stop_all)) if (ti == 0 and pc == 0) else ())
                def ones_ev(e, t2=ts):
                    dst = vtok(t2)[:, :].rearrange("p (h s) -> p h s", h=16)[:, :, 64:66]
                    src = onet[:, 0:32].rearrange("p (h s) -> p h s", h=16)
                    return e.tensor_scalar(R(dst), src, 0.0, 1.0, ALU.mult, ALU.add)
                vop(ones_ev)

        def vproj_filler(w_):
            used_t = [S["t"], S["t"]]

            def filler(fc):
                wb = wblk if fc % 2 == 0 else wblk2
                ws = "wb" if fc % 2 == 0 else "wbB"
                qn = "sync" if fc % 2 == 0 else "gpsimd"
                ld = dma(qn, ws, R(wb[:]), R(wv[fc * P:(fc + 1) * P, :]),
                         wait=W(("t", used_t[fc % 2])))
                for ti in range(2):
                    ts = 2 * w_ + ti
                    for pc in range(2):
                        w0 = W((ws, ld)) if (ti == 0 and pc == 0) else ()
                        if fc == 0 and ti == 0 and pc == 0:
                            w0 = w0 + (("v", S["v"]), ("s", S["s"]))
                        mm(uA(fc)[:, ts * P:(ts + 1) * P],
                           wb[:, pc * TT:(pc + 1) * TT],
                           psb[ti * 2 + pc][:, :], fc == 0, fc == FC - 1, wait=w0)
                used_t[fc % 2] = S["t"]
            return filler

        qfv = fm(qfp, kv_, extra_wait=W(("sq", qf_spill_st), ("s", ks_)),
                 filler=vproj_filler(0))
        stopA = S["t"]
        vproj_wave_evacs(0, stopA)
        for fcq in range(FC):
            qf_spill_st = dma("gpsimd", "sq", R(sp_qf[j, fcq]), R(qfp(fcq)),
                              wait=W(("v", qfv)))
        kfv = fm(kfp, qfv, src=kfp, extra_wait=W(("s", ks_)),
                 filler=vproj_filler(1))
        stopB = S["t"]
        vproj_wave_evacs(1, stopB)
        vtok_v = S["v"]

        tb_done = [("v", S["v"]), ("v", S["v"]), ("v", S["v"]), ("s", S["s"])]
        for s_ in range(4):
            for pr in range(FC):
                bi = (s_ * FC + pr) % 4
                bank = psb[2 + bi]
                t_ = tp(bank[:, 0:P], kfp(pr)[:, s_ * P:(s_ + 1) * P], P,
                        wait=W(tb_done[bi]))
                if bi < 2:
                    vop(lambda e, b_=bank, d=ktk(s_)[:, pr * P:(pr + 1) * P]:
                        e.tensor_copy(R(d), b_[:, 0:P]), wait=W(("t", t_)))
                    tb_done[bi] = ("v", S["v"])
                else:
                    sop(lambda e, b_=bank, d=ktk(s_)[:, pr * P:(pr + 1) * P]:
                        e.activation(R(d), b_[:, 0:P], AF.Copy), wait=W(("t", t_)))
                    tb_done[bi] = ("s", S["s"])
        ktk_v = S["v"]
        ktk_s = S["s"]

        def state_filler(h):
            bi = (2 + h // 8) if h % 2 == 0 else (4 + h // 8)
            cc0 = ((h % 8) // 2) * 66
            for s_ in range(4):
                first = (h in (0, 1, 8, 9)) and s_ == 0
                last = (h in (6, 7, 14, 15)) and s_ == 3
                w0 = W(("v", max(vtok_v, ktk_v)), ("s", ktk_s)) \
                    if h in (0, 1) and s_ == 0 else ()
                mm(ktk(s_)[:, h * D:(h + 1) * D], vsl66(s_, h),
                   psb[bi][0:D, cc0:cc0 + 66], first, last, wait=w0)

        abank_free = [(S["v"], S["s"])] * 2
        ndbank_free = [S["s"]] * 2
        atm_done = [(0, 0)] * H
        nd_mm_done = [0] * H

        def emit_A(hh):
            for s_ in range(4):
                bank = psb[s_ % 2]
                av, asv = abank_free[s_ % 2]
                a_ = mm(kfp(hh // 2)[(hh % 2) * D:(hh % 2) * D + D, s_ * P:(s_ + 1) * P],
                        hsl(qft, hh)[:, s_ * P:TT], bank[:, 0:AW[s_]], True, True,
                        wait=W(("v", av), ("s", asv)))
                mv_ = vop(lambda e, b_=bank, hh2=hh, c0=AOFF[s_]: e.tensor_tensor(
                    R(atm2(hh2, c0, P)), b_[:, 0:P], mskt[:, :], ALU.mult),
                    wait=W(("t", a_)))
                if AW[s_] > P:
                    sop(lambda e, b_=bank, hh2=hh, c0=AOFF[s_], wd=AW[s_]: e.activation(
                        R(atm2(hh2, c0 + P, wd - P)), b_[:, P:wd], AF.Copy),
                        wait=W(("v", mv_)))
                abank_free[s_ % 2] = (S["v"], S["s"])
            atm_done[hh] = (S["v"], S["s"])

        def emit_nd(hh):
            nb = psb[6 + hh % 2]
            for s_ in range(4):
                if s_ == 0:
                    w0 = W(("v", atm_done[hh][0]), ("s", atm_done[hh][1]),
                           ("s", max(ndbank_free[hh % 2], atm_done[hh][1])))
                    w0 = tuple({k: v for k, v in w0}.items())
                else:
                    w0 = ()
                stop = (j == 0 and s_ == 3)
                mm(vsl(s_, hh), atm2(hh, AOFF[s_], AW[s_]), nb[0:65, s_ * P:TT],
                   s_ == 0, stop, wait=w0)
            if j > 0:
                mm(ssl(sPr, hh), hsl(qft, hh), nb[0:65, :], False, True)
            nd_mm_done[hh] = S["t"]
            pr = hh // 2
            sop(lambda e, b_=nb, hh2=hh: e.activation(
                R(nst(hh2 // 2 % 2)[(hh2 % 2) * D:(hh2 % 2) * D + D, 0:TT]),
                b_[0:D, :], AF.Copy),
                wait=W(("t", nd_mm_done[hh]),
                       ("snA" if pr % 2 == 0 else "snB", nm_spill_st[pr % 2])))
            sop(lambda e, b_=nb, hh2=hh: e.activation(
                R(nst(hh2 // 2 % 2)[(hh2 % 2) * D:(hh2 % 2) * D + 1, TT:2 * TT]),
                b_[D:65, :], AF.Copy))
            ndbank_free[hh % 2] = S["s"]
            if hh % 2 == 1:
                nm_spill_st[pr % 2] = dma(
                    "gpsimd", "snA" if pr % 2 == 0 else "snB", R(sp_nm[j, pr]),
                    R(nst(pr % 2)[:, :]), wait=W(("s", S["s"])))

        pre_nd_t = S["t"]
        stj_box = [0]
        emit_A(0)
        for h in range(H):
            if h + 1 < H:
                emit_A(h + 1)
            state_filler(h)
            if h == H - 1:
                tmm = S["t"]
                vop(lambda e: e.tensor_copy(R(sA[0:D, 0:264]), psb[2][0:D, 0:264]),
                    wait=W(("t", tmm), ("v", S["v"])))
                vop(lambda e: e.tensor_copy(R(sA[D:P, 0:264]), psb[4][0:D, 0:264]))
                vop(lambda e: e.tensor_copy(R(sA[0:D, 264:528]), psb[3][0:D, 0:264]))
                vop(lambda e: e.tensor_copy(R(sA[D:P, 264:528]), psb[5][0:D, 0:264]))
                stj_box[0] = S["v"]
            emit_nd(h)
        stj_v = stj_box[0]
        prev_tail_v = S["v"]
        prev_tail_s = max(ndbank_free)

        if j == 0:
            vop(lambda e: e.tensor_copy(R(sPr[:, :]), sA[:, :]),
                wait=W(("v", stj_v), ("t", S["t"])))
        elif j < NT - 1:
            vop(lambda e: e.tensor_tensor(R(sPr[:, :]), sPr[:, :], sA[:, :], ALU.add),
                wait=W(("v", stj_v), ("t", S["t"])))
        else:
            total_v = vop(lambda e: e.tensor_tensor(R(gR[:, :]), sPr[:, :], sA[:, :],
                                                    ALU.add),
                          wait=W(("v", stj_v), ("t", pre_nd_t)))
    total = gR

    dma("gpsimd", "sc", R(cc_in[:, :]), R(total[:, :]), wait=W(("v", total_v)))
    st_now = S["sc"]
    emit("gpsimd", lambda e: e.collective_compute(
        "AllGather", ALU.bypass, replica_groups=[list(range(NC))],
        ins=[R(cc_in[:, :])], outs=[R(cc_out[:, :, :])]), wait=W(("sc", st_now)), inc="cc")
    cc_val = S["cc"]
    p1_end_t = S["t"]
    p1_end_s = S["s"]
    slg = lambda c: big0[:, 4096 + c * 528:4096 + (c + 1) * 528]
    for c_ in range(NC):
        w0 = W(("cc", cc_val), ("t", p1_end_t), ("s", p1_end_s)) if c_ == 0 \
            else W(("cc", cc_val))
        dma("gpsimd", "sc", R(slg(c_)), R(cc_out[c_, :, :]), wait=w0)
    dz = S["sc"]
    for c_ in range(NC):
        vop(lambda e, c2=c_: e.tensor_scalar(R(slg(c2)), slg(c2), ccft[:, c2:c2 + 1],
                                             0.0, ALU.mult, ALU.bypass),
            wait=W(("sc", dz)) if c_ == 0 else ())
    b1 = bounce("scalar", "s", W(("v", S["v"])))
    for p_ in range(4):
        vop(lambda e, p2=p_: e.tensor_tensor(R(slg(2 * p2)), slg(2 * p2),
                                             slg(2 * p2 + 1), ALU.add),
            wait=W(("s", b1)) if p_ == 0 else ())
    b2 = bounce("scalar", "s", W(("v", S["v"])))
    for p_ in range(2):
        vop(lambda e, p2=p_: e.tensor_tensor(R(slg(4 * p2)), slg(4 * p2),
                                             slg(4 * p2 + 2), ALU.add),
            wait=W(("s", b2)) if p_ == 0 else ())
    b3 = bounce("scalar", "s", W(("v", S["v"])))
    vop(lambda e: e.tensor_tensor(R(gB[:, :]), slg(0), slg(4), ALU.add),
        wait=W(("s", b3)))
    gb_v = S["v"]

    p2_tail_t = S["t"]
    p2_tail_v = S["v"]
    wfree_wo = [p1_end_t] * FC
    wo_evac = [S["v"]] * 4
    p2_tile_done = [0] * NT
    p2_entry_s = S["s"]
    rl_carry = [0]
    for j in range(NT):
        def nmr(pr):
            return big1[:, pr * DM:(pr + 1) * DM] if pr < 4 else \
                   big0[:, (pr - 4) * DM:(pr - 3) * DM]
        if j == 0:
            w0 = W(("t", p1_end_t), ("v", total_v), ("sq", S["sq"]),
                   ("snA", S["snA"]), ("snB", S["snB"]))
            for f2 in range(FC):
                dma("sync", "rl", R(qfp(f2)), R(sp_qf[j, f2]),
                    wait=w0 if f2 == 0 else ())
            for p2 in range(FC):
                dma("sync", "rl", R(nmr(p2)), R(sp_nm[j, p2]))
            rl_all = ("rl", S["rl"])
        else:
            rl_all = ("rlB", rl_carry[0])
        ndfree = [gb_v, gb_v]
        bkfree = [gb_v, gb_v]
        attp = [0] * FC
        denv = 0
        for h in range(H):
            nb = psb[h % 2]
            pr = h // 2
            par = h % 2
            gmm = mm(ssl(gB, h), hsl(qft, h), nb[0:65, :], True, True,
                     wait=W(rl_all, ("v", max(gb_v, ndfree[h % 2]))))
            vop(lambda e, b_=nb, p2=pr, pa=par: e.scalar_tensor_tensor(
                R(recrow(pa)), b_[D:65, :], 1e-6,
                nmr(p2)[pa * D:pa * D + 1, TT:2 * TT], ALU.add, ALU.add),
                wait=W(("t", gmm)))
            denv = S["v"]
            vop(lambda e, b_=nb, p2=pr, pa=par: e.tensor_tensor(
                R(qk(p2)[pa * D:pa * D + D, :]), b_[0:D, :],
                nmr(p2)[pa * D:pa * D + D, 0:TT], ALU.add))
            ndfree[h % 2] = S["v"]
            if par == 1:
                bk = psb[2 + pr % 2]
                mm(onet[0:1, 0:D], recrow(0), bk[0:D, :], True, True,
                   wait=W(("v", max(denv, bkfree[pr % 2]))))
                bmm = mm(onet[64:65, 0:D], recrow(1), bk[D:P, :], True, True, dt=F32)
                vop(lambda e, b2=bk: e.reciprocal(R(scr2), b2[:, :]),
                    wait=W(("t", bmm)))
                bp = bounce("scalar", "s", W(("v", S["v"])))
                vop(lambda e, p2=pr: e.tensor_tensor(
                    R(qk(p2)), qk(p2), scr2, ALU.mult), wait=W(("s", bp)))
                bkfree[pr % 2] = S["v"]
                attp[pr] = S["v"]
                if j + 1 < NT:
                    dma("gpsimd", "rlB", R(qfp(pr)), R(sp_qf[j + 1, pr]),
                        wait=W(("t", gmm)))
                    dma("gpsimd", "rlB", R(nmr(pr)), R(sp_nm[j + 1, pr]),
                        wait=W(("v", ndfree[h % 2])))
        rl_carry[0] = S["rlB"]
        att_v = S["v"]
        p2_tail_t = S["t"]
        p2_tail_v = S["v"]
        for half in range(2):
            tg = load_w_grouped(wo, half, wfree_wo)
            stops = [None] * 4
            for fc in range(FC):
                for oc_ in range(4):
                    w0 = ()
                    if oc_ == 0 and fc % 2 == 0:
                        w0 = W(tg[fc // 2])
                    if oc_ == 0:
                        w0 = w0 + (("v", attp[fc]),)
                    if fc == 0:
                        w0 = w0 + (("v", wo_evac[oc_]),)
                        if j == 0 and half == 0 and oc_ == 0:
                            w0 = w0 + (("s", p2_entry_s),)
                    m_ = mm(wres[:, fc * TT + oc_ * P:fc * TT + oc_ * P + P], qk(fc),
                            psb[4 + oc_][:, :], fc == 0, fc == FC - 1, wait=w0)
                    if fc == FC - 1:
                        stops[oc_] = m_
                wfree_wo[fc] = S["t"]
            for oc_ in range(4):
                oc = half * 4 + oc_
                wo_evac[oc_] = vop(
                    lambda e, o=strm[oc][:, j * TT:(j + 1) * TT], b_=psb[4 + oc_]:
                    e.tensor_tensor(R(o), o, b_[:, :], ALU.add),
                    wait=W(("t", stops[oc_])))
        p2_tile_done[j] = S["v"]

    sv3 = ln_stats(tile_waits=[max(p2_tile_done[t_], att_v) for t_ in range(NT)],
                   xs_fn=lambda par: big3[:, 1280 + par * TT:1280 + (par + 1) * TT])
    td3 = ffn(sv3, setup_banks=(4, 5, 6, 7), setup_guard=p2_tile_done[NT - 1])

    out_st = [0, 0]
    obank_s = [0, 0, 0, 0]
    first_o = True
    for it in range(16):
        stg = tstage[it % 2]
        osem = "soA" if it % 2 == 0 else "soB"
        for fc in range(FC):
            bi = fc % 4
            bank = psb[bi]
            w0 = W(("s", obank_s[bi]))
            if fc == 0 and it % 4 == 0:
                w0 = w0 + (("v", td3[it // 4]),)
            if first_o:
                w0 = w0 + (("s", S["s"]),)
                first_o = False
            t_ = tp(bank[:, 0:P], strm[fc][:, it * P:(it + 1) * P], P, wait=w0)
            w0 = W(("t", t_))
            if fc == 0:
                w0 = w0 + ((osem, out_st[it % 2]),)
            sop(lambda e, b_=bank, d=stg[:, fc * P:(fc + 1) * P]:
                e.activation(R(d), b_[:, 0:P], AF.Copy), wait=w0)
            obank_s[bi] = S["s"]
        out_st[it % 2] = dma("gpsimd", osem, out[it * P:(it + 1) * P, :], stg,
                             wait=W(("s", S["s"])))
    emit("gpsimd", lambda e: e.nop(), wait=W(("soA", out_st[0]), ("soB", out_st[1])))

    with nc.Block() as block:
        def runner(name):
            def _f(eng):
                for (waits, build, inc, inc_n) in OPS[name]:
                    for (sname, val) in waits:
                        if val > 0:
                            eng.wait_ge(sems[sname], val)
                    ins = build(eng)
                    if inc is not None:
                        if ins is None:
                            eng.sem_inc(sems[inc], inc_n)
                        else:
                            ins.then_inc(sems[inc], inc_n)
            return _f
        block.tensor(runner("tensor"))
        block.vector(runner("vector"))
        block.scalar(runner("scalar"))
        block.sync(runner("sync"))
        block.gpsimd(runner("gpsimd"))

    es.close()
    return nc


def kernel(L0_attn_x, L0_attn_Wq, L0_attn_Wk, L0_attn_Wv, L0_attn_Wo, L0_attn_proj,
           L0_ffn_base_w, L0_ffn_base_b, L0_ffn_lora_A, L0_ffn_lora_B,
           L0_ffn_router_w, L0_ffn_router_b):
    f32 = lambda a: np.ascontiguousarray(np.asarray(a, dtype=np.float32))
    x = f32(L0_attn_x)
    B, Sq, _ = x.shape
    wq, wk, wv, wo = f32(L0_attn_Wq), f32(L0_attn_Wk), f32(L0_attn_Wv), f32(L0_attn_Wo)
    proj = f32(L0_attn_proj)
    bwT = np.ascontiguousarray(f32(L0_ffn_base_w).T)
    bb8 = np.ascontiguousarray(f32(L0_ffn_base_b).reshape(1, DM))
    ra = np.ascontiguousarray(np.transpose(f32(L0_ffn_lora_A).reshape(8, P, 8),
                                           (1, 0, 2)).reshape(P, 64))
    rwp = np.ascontiguousarray(np.transpose(f32(L0_ffn_router_w).T.reshape(8, P, 8),
                                            (1, 0, 2)).reshape(P, 64))
    rb = f32(L0_ffn_router_b).reshape(8, 1)
    lbf = np.ascontiguousarray(f32(L0_ffn_lora_B).reshape(64, DM))

    mskT = np.triu(np.ones((P, P), dtype=np.float32))
    idn = np.eye(P, dtype=np.float32)
    onesw = np.ones((P, TT), dtype=np.float32)
    hfc = np.full((P, P), 0.5, dtype=np.float32)
    repg = np.zeros((40, 64), dtype=np.float32)
    repm = np.zeros((40, 64), dtype=np.float32)
    for c in range(64):
        for r0 in (0, 32):
            repg[r0 + c // 8, c] = 1.0
            repm[r0 + c % 8, c] = 1.0
    cst = np.zeros((P, 8), dtype=np.float32)
    cst[:, C_EPS_LN] = 1e-5
    cst[:, C_EPS_DEN] = 1e-6
    cst[:, C_NLN2] = -LN2
    cst[:, C_PLN2] = LN2
    cst[:, C_M7LN2] = -7.0 * LN2

    nc = build_nc()
    in_maps = []
    for c in range(NC):
        b, r = c // 4, c % 4
        ccf = np.zeros((P, 8), dtype=np.float32)
        for c2 in range(NC):
            if c2 // 4 == b and c2 % 4 < r:
                ccf[:, c2] = 1.0
        in_maps.append({
            "xin": np.ascontiguousarray(x[b, r * T:(r + 1) * T, :]),
            "hf": hfc,
            "wq": wq, "wk": wk, "wv": wv, "wo": wo, "bwT": bwT, "bb8": bb8,
            "ra": ra, "rwp": rwp, "rb": rb, "lb": lbf,
            "prj": np.ascontiguousarray(np.vstack([proj, proj])),
            "mskT": mskT, "idn": idn, "onesw": onesw, "repg": repg, "repm": repm,
            "cst": cst, "ccf": ccf,
        })
    res = run_bass_kernel_spmd(nc, in_maps, core_ids=list(range(NC)))
    y = np.empty((B, Sq, DM), dtype=np.float32)
    for c in range(NC):
        b, r = c // 4, c % 4
        y[b, r * T:(r + 1) * T, :] = res.results[c]["out"]
    return y


if __name__ == "__main__":
    import sys
    if "--build" in sys.argv:
        build_nc()
        print("build ok")
